# version 01
# baseline (speedup 1.0000x reference)
import math
import numpy as np
import concourse.bass as bass
import concourse.mybir as mybir
from concourse.bass_utils import run_bass_kernel_spmd

F32 = mybir.dt.float32
BF16 = mybir.dt.bfloat16
F16 = mybir.dt.float16
AF = mybir.ActivationFunctionType
ALU = mybir.AluOpType
AX = mybir.AxisListType

TOK, D = 2048, 1024
NT, NCH, NQT = 16, 8, 4
HEADS, NPAIR = 16, 8
EPS = 1e-5
MAGIC = 12582912.0

N_BLOCKS = 4
DEBUG_TAPS = ()


def alibi_slopes16():
    def pow2(n):
        start = 2 ** (-(2 ** (-(math.log2(n) - 3))))
        return [start ** i for i in range(n)]
    return np.asarray(pow2(16), dtype=np.float32)


SLOPES = [float(s) for s in alibi_slopes16()]
SKIP_THRESH = 60.0


def band_minabs(kt, qt):
    d0 = 128 * kt - 512 * qt
    lo, hi = d0 - 511, d0 + 127
    if lo <= 0 <= hi:
        return 0
    return min(abs(lo), abs(hi))


def pair_sched(pair, l1):
    out = []
    for qt in range(NQT):
        kl = []
        for ktp in range(8):
            kts = [2 * ktp, 2 * ktp + 1]
            if l1:
                hA, hB = 2 * pair, 2 * pair + 1
                keep = any(SLOPES[h] * band_minabs(kt, qt) < SKIP_THRESH
                           for h in (hA, hB) for kt in kts)
            else:
                keep = True
            if keep:
                kl.append(ktp)
        out.append(kl)
    return out


class Ctx:
    def __init__(self, nc):
        self.nc = nc
        self.sem = {}
        self.cnt = {}
        self._guards = []

    def S(self, name):
        if name not in self.sem:
            g = self.nc.semaphore(name)
            self.sem[name] = g.__enter__()
            self._guards.append(g)
            self.cnt[name] = 0
        return self.sem[name]

    def inc(self, ins, name, k=1):
        ins.then_inc(self.S(name), k)
        self.cnt[name] += k
        return self.cnt[name]

    def val(self, name):
        self.S(name)
        return self.cnt[name]


def build(n_blocks=N_BLOCKS, taps=()):
    nc = bass.Bass("TRN2", target_bir_lowering=False)
    P = Ctx(nc)

    x_in = nc.dram_tensor("x", [TOK, D], F32, kind="ExternalInput")
    w_down = nc.dram_tensor("w_down", [D, D], F32, kind="ExternalInput")
    tokeys = nc.dram_tensor("tokeys", [HEADS, 64, 64], F32, kind="ExternalInput")
    toquer = nc.dram_tensor("toquer", [HEADS, 64, 64], F32, kind="ExternalInput")
    tovals = nc.dram_tensor("tovals", [HEADS, 64, 64], F32, kind="ExternalInput")
    w_uni = nc.dram_tensor("w_uni", [D, D], F32, kind="ExternalInput")
    b_uni = nc.dram_tensor("b_uni", [D], F32, kind="ExternalInput")
    wg1 = nc.dram_tensor("wg1", [D, 256], F32, kind="ExternalInput")
    bg1 = nc.dram_tensor("bg1", [256], F32, kind="ExternalInput")
    wg2 = nc.dram_tensor("wg2", [256, 8], F32, kind="ExternalInput")
    w_ine = nc.dram_tensor("w_ine", [8, D, 2048], F32, kind="ExternalInput")
    w_oute = nc.dram_tensor("w_oute", [8, 2048, D], F32, kind="ExternalInput")
    wqkv = nc.dram_tensor("wqkv", [D, 3 * D], F32, kind="ExternalInput")
    bqkv = nc.dram_tensor("bqkv", [3 * D], F32, kind="ExternalInput")
    wout_d = nc.dram_tensor("wout", [D, D], F32, kind="ExternalInput")
    bout = nc.dram_tensor("bout", [D], F32, kind="ExternalInput")
    w1d = nc.dram_tensor("w1", [4096, D], F32, kind="ExternalInput")
    convw = nc.dram_tensor("convw", [4096, 1, 3], F32, kind="ExternalInput")
    convb = nc.dram_tensor("convb", [4096], F32, kind="ExternalInput")
    w2d = nc.dram_tensor("w2", [D, 4096], F32, kind="ExternalInput")
    ident_in = nc.dram_tensor("ident", [128, 128], BF16, kind="ExternalInput")
    identf_in = nc.dram_tensor("identf", [2, 2], F32, kind="ExternalInput")
    dmast = nc.dram_tensor("dmast", [128, 28 * 512], F16, kind="ExternalInput")

    out_d = nc.dram_tensor("out", [TOK, D], F32, kind="ExternalOutput")
    x_state = nc.dram_tensor("x_state", [TOK, D], F32)
    w2t_spill = nc.dram_tensor("w2t_spill", [128, 32 * 1024], BF16, kind="ExternalOutput" if "w2t" in taps else "Internal")

    tapd = {}
    for t in taps:
        if t == 'w2t':
            continue
        shp, dt_ = {
            "xn_tm": ([128, NT * D], BF16), "xn_fm": ([128, NCH * TOK], BF16),
            "xd_fm": ([128, NCH * TOK], BF16), "av_fm": ([128, NCH * TOK], BF16),
            "kp": ([128, TOK], BF16), "qp": ([128, TOK], BF16), "vp": ([128, TOK], BF16),
            "gates": ([128, NT * 8], F32), "u_dbg": ([128, NT * D], F32),
            "h_dbg": ([128, 16 * 512], BF16), "hs_tm": ([128, NT * 4096], BF16),
            "sm1": ([128, 384], F32), "t_dbg": ([128, 4096], F32),
            "smw": ([128, 384], F32), "hsC": ([128, 16384], BF16), "xqA": ([128, 16384], BF16),
            "xq2C": ([128, 16384], BF16), "yC": ([128, 8192], F32),
            "hbq": ([128, 6144], BF16), "ff": ([128, 4096], F32),
        }[t]
        tapd[t] = nc.dram_tensor("tap_" + t, shp, dt_, kind="ExternalOutput")

    CSB = nc.alloc_sbuf_tensor_at("CSBbf", [128, 2816], BF16, offset=16896)
    CST = nc.alloc_sbuf_tensor_at("CSTf32", [128, 512], F32, offset=22528)
    SM = nc.alloc_sbuf_tensor_at("SMf32", [128, 384], F32, offset=24576)
    ONF = nc.alloc_sbuf_tensor_at("ONF", [128, 128], F32, offset=26112)
    IDF = nc.alloc_sbuf_tensor_at("IDF2", [2, 16], F32, offset=26624)
    VT = nc.alloc_sbuf_tensor_at("VTbf", [128, 2816], BF16, offset=26688)
    Hb = nc.alloc_sbuf_tensor_at("Hbf", [128, 6144], BF16, offset=32320)
    Hbf32 = nc.alloc_sbuf_tensor_at("Hbf32", [128, 512], F32, offset=36480)
    H = nc.alloc_sbuf_tensor_at("Hf32", [128, 1024], F32, offset=44608)
    A = nc.alloc_sbuf_tensor_at("Abf", [128, 16384], BF16, offset=48704)
    G = nc.alloc_sbuf_tensor_at("Gf32", [128, 16384], F32, offset=48704)
    DM = nc.alloc_sbuf_tensor_at("DMf16", [128, 14336], F16, offset=48704)
    Bb = nc.alloc_sbuf_tensor_at("Bbf", [128, 16384], BF16, offset=81472)
    YS0 = nc.alloc_sbuf_tensor_at("ysb0", [128, 8192], F32, offset=81472)
    C = nc.alloc_sbuf_tensor_at("Cbf", [128, 16384], BF16, offset=114240)
    CF32 = nc.alloc_sbuf_tensor_at("Cf32", [128, 8192], F32, offset=114240)
    Dd = nc.alloc_sbuf_tensor_at("Dbf", [128, 16384], BF16, offset=147008)
    DF32 = nc.alloc_sbuf_tensor_at("Df32", [128, 8192], F32, offset=147008)
    E = nc.alloc_sbuf_tensor_at("Ebf", [128, 16384], BF16, offset=179776)
    Ef32 = nc.alloc_sbuf_tensor_at("Ef32", [128, 8192], F32, offset=179776)
    YS1 = nc.alloc_sbuf_tensor_at("ysb1", [128, 8192], F32, offset=179776)
    Fb = nc.alloc_sbuf_tensor_at("Fbf", [128, 8192], BF16, offset=212544)
    Ff32 = nc.alloc_sbuf_tensor_at("Ff32", [128, 4096], F32, offset=212544)

    IDB, TOKT = 0, 128
    E2B = TOKT + 1536
    WG2B = E2B + 2
    BUNIB = WG2B + 16
    ONES2, E2T = 0, 128
    BQKV = E2T + 128
    BG1C = BQKV + 24
    CONVW = BG1C + 2
    CONVB = CONVW + 96
    S_SUM, S_SQ, S_MEAN, S_VAR, S_R, S_NMR, S_TMP = 0, 16, 32, 48, 64, 80, 96
    S_QM, S_C2, S_NC = 112, 120, 124
    S_SCL = 128
    S_ST2 = 144
    S_WS = 160
    S_G = 176
    S_R2, S_NMR2 = 304, 320
    S_LGT = 336

    def ap(t, c0, w, p0=0, pn=128):
        return t[p0:p0 + pn, c0:c0 + w]

    ps = [nc.alloc_psum_tensor(f"ps{i}", [128, 512], F32) for i in range(6)]
    pt0 = nc.alloc_psum_tensor("pt0", [128, 1024], BF16)
    pt1 = nc.alloc_psum_tensor("pt1", [128, 1024], BF16)

    SY, GQ, MM, EV, AC, ST = "syq", "gq", "mm", "ev", "ac", "st"

    ncd = nc.allow_non_contiguous_dma(reason="small constant loads")
    ncd.__enter__()
    with nc.Block() as blk:
        @blk.gpsimd
        def _(g):
            P.inc(g.dma_start(ap(CSB, IDB, 128), ident_in[:, :]), GQ, 16)
            P.inc(g.dma_start(IDF[0:2, 0:2], identf_in[:, :]), GQ, 16)
            for pi, T in enumerate((tokeys, toquer, tovals)):
                for h in range(HEADS):
                    pair, sub = h // 2, h % 2
                    col = TOKT + (pi * 8 + pair) * 64
                    src = bass.AP(T, h * 4096, [[1, 64], [64, 64]])
                    P.inc(g.dma_start(ap(CSB, col, 64, p0=sub * 64, pn=64), src,
                                      ), GQ, 16)
            g.memset(ap(CSB, E2B, 2), 0.0)
            g.memset(ap(CSB, E2B, 1, p0=0, pn=64), 1.0)
            g.memset(ap(CSB, E2B + 1, 1, p0=64, pn=64), 1.0)
            for c in range(2):
                src = bass.AP(wg2, c * 128 * 8, [[8, 128], [1, 8]])
                P.inc(g.dma_start(ap(CSB, WG2B + c * 8, 8), src), GQ, 16)
            g.memset(ONF[:, :], 1.0)
            g.memset(ap(CST, ONES2, 128, pn=2), 1.0)
            P.inc(g.dma_start(ap(CST, BQKV, 24), bqkv.ap().rearrange("(c p) -> p c", p=128)), GQ, 16)
            P.inc(g.dma_start(ap(CST, BG1C, 2), bg1.ap().rearrange("(c p) -> p c", p=128)), GQ, 16)
            src = bass.AP(convw, 0, [[3, 128], [384, 32], [1, 3]])
            P.inc(g.dma_start(ap(CST, CONVW, 96), src), GQ, 16)
            P.inc(g.dma_start(ap(CST, CONVB, 32), convb.ap().rearrange("(c p) -> p c", p=128)), GQ, 16)
            g.memset(VT[:, :], 0.0)
            for kt in range(16):
                g.memset(ap(VT, kt * 80 + 64, 1), 1.0)
            for k in range(NCH):
                P.inc(g.dma_start(ap(Fb, k * 1024, 1024), w_down[k * 128:(k + 1) * 128, :]), GQ, 16)
            g.wait_ge(P.S(GQ), P.val(GQ))
        @blk.sync
        def _(sy):
            P.inc(sy.dma_start(ap(H, 0, 1024, pn=1), b_uni.ap()[None, :]), SY, 16)
    ncd.__exit__(None, None, None)

    with nc.Block() as blk:
        @blk.tensor
        def _(t):
            t.wait_ge(P.S(SY), P.val(SY))
            for dt_ in range(2):
                ins = t.matmul(ps[dt_][:, :], ONF[0:1, :], ap(H, dt_ * 512, 512, pn=1),
                               start=True, stop=True, tile_position=(0, 0))
            P.inc(ins, MM)
        @blk.vector
        def _(v):
            v.wait_ge(P.S(MM), P.val(MM))
            for dt_ in range(2):
                v.tensor_copy(ap(CSB, BUNIB + dt_ * 512, 512), ps[dt_][:, :])

    def emit_ln(src_dram, quant=False):
        with nc.Block() as blk:
            base_sy = P.val(SY)
            @blk.sync
            def _(sy):
                for t in range(NT):
                    P.inc(sy.dma_start(ap(G, t * 1024, 1024), src_dram[t * 128:(t + 1) * 128, :]), SY, 16)
            @blk.vector
            def _(v):
                for t in range(NT):
                    v.wait_ge(P.S(SY), base_sy + 16 * (t + 1))
                    ins = v.reduce_sum(ap(SM, S_SUM + t, 1), ap(G, t * 1024, 1024), AX.X)
                P.inc(ins, EV)
            @blk.scalar
            def _(s):
                for t in range(NT):
                    s.wait_ge(P.S(SY), base_sy + 16 * (t + 1))
                    ins = s.activation(ap(H, 0, 1024), ap(G, t * 1024, 1024), AF.Square,
                                       accum_out=ap(SM, S_SQ + t, 1))
                P.inc(ins, AC)
        with nc.Block() as blk:
            @blk.vector
            def _(v):
                v.tensor_scalar(ap(SM, S_MEAN, 16), ap(SM, S_SUM, 16), 1.0 / D, 0.0, ALU.mult, ALU.add)
                v.drain()
                v.tensor_tensor(ap(SM, S_TMP, 16), ap(SM, S_MEAN, 16), ap(SM, S_MEAN, 16), ALU.mult)
                v.drain()
                v.scalar_tensor_tensor(ap(SM, S_VAR, 16), ap(SM, S_SQ, 16), 1.0 / D, ap(SM, S_TMP, 16),
                                       ALU.mult, ALU.subtract)
                v.drain()
                v.tensor_scalar(ap(SM, S_VAR, 16), ap(SM, S_VAR, 16), EPS, 0.0, ALU.add, ALU.add)
                v.drain()
                ins = v.reciprocal(ap(SM, S_TMP, 16), ap(SM, S_VAR, 16))
                P.inc(ins, EV)
            @blk.scalar
            def _(s):
                s.wait_ge(P.S(EV), P.val(EV))
                ins = s.activation(ap(SM, S_R, 16), ap(SM, S_TMP, 16), AF.Sqrt)
                P.inc(ins, AC)
        with nc.Block() as blk:
            @blk.vector
            def _(v):
                ins = v.scalar_tensor_tensor(ap(SM, S_NMR, 16), ap(SM, S_MEAN, 16), -1.0,
                                             ap(SM, S_R, 16), ALU.mult, ALU.mult)
                P.inc(ins, EV)
        if not quant:
            with nc.Block() as blk:
                @blk.scalar
                def _(s):
                    for t in range(NT):
                        ins = s.activation(ap(C, t * 1024, 1024), ap(G, t * 1024, 1024), AF.Identity,
                                           bias=ap(SM, S_NMR + t, 1), scale=ap(SM, S_R + t, 1))
                    P.inc(ins, AC)
        else:
            with nc.Block() as blk:
                base_ac = P.val(AC)
                @blk.scalar
                def _(s):
                    for t in range(NT):
                        ins = s.activation(ap(G, t * 1024, 1024), ap(G, t * 1024, 1024), AF.Identity,
                                           bias=ap(SM, S_NMR + t, 1), scale=ap(SM, S_R + t, 1))
                        P.inc(ins, AC)
                @blk.vector
                def _(v):
                    for t in range(NT):
                        v.wait_ge(P.S(AC), base_ac + t + 1)
                        v.tensor_reduce(ap(SM, S_TMP + t, 1), ap(G, t * 1024, 1024), AX.X,
                                        ALU.max, apply_absolute_value=True)
                    v.drain()
                    v.tensor_scalar(ap(SM, S_TMP, 16), ap(SM, S_TMP, 16), 1e-5, 0.0, ALU.max, ALU.add)
                    v.drain()
                    v.tensor_scalar(ap(SM, S_SCL, 16), ap(SM, S_TMP, 16), 1.0 / 127.0, 0.0, ALU.mult, ALU.add)
                    v.drain()
                    v.reciprocal(ap(SM, S_ST2, 16), ap(SM, S_SCL, 16))
                    v.drain()
                    for t in range(NT):
                        v.tensor_scalar(ap(H, 0, 1024), ap(G, t * 1024, 1024), ap(SM, S_ST2 + t, 1),
                                        MAGIC, ALU.mult, ALU.add)
                        v.drain()
                        v.tensor_scalar(ap(H, 0, 1024), ap(H, 0, 1024), MAGIC, 0.0,
                                        ALU.subtract, ALU.add)
                        v.drain()
                        v.tensor_scalar(ap(C, t * 1024, 1024), ap(H, 0, 1024), ap(SM, S_SCL + t, 1),
                                        1.0, ALU.mult, ALU.mult)
                        v.drain()
                    P.inc(v.engine_nop(), EV)

    def emit_transpose_tm_fm(src, src_base, row_w, dst, dst_base, tok_w, nt, nch_):
        with nc.Block() as blk:
            base_ev = P.val(EV)
            @blk.tensor
            def _(t):
                for ti in range(nt):
                    pt = (pt0, pt1)[ti % 2]
                    if ti >= 2:
                        t.wait_ge(P.S(EV), base_ev + ti - 1)
                    for c in range(nch_):
                        ins = t.transpose(pt[:, c * 128:(c + 1) * 128],
                                          ap(src, src_base + ti * row_w + c * 128, 128),
                                          ap(CSB, IDB, 128))
                    P.inc(ins, MM)
            base_mm = P.val(MM) - nt
            @blk.vector
            def _(v):
                for ti in range(nt):
                    pt = (pt0, pt1)[ti % 2]
                    v.wait_ge(P.S(MM), base_mm + ti + 1)
                    dst_ap = bass.AP(dst, dst_base + ti * 128,
                                     [[dst.shape[1], 128], [tok_w, nch_], [1, 128]])
                    ins = v.tensor_copy(dst_ap, pt[:, 0:nch_ * 128])
                    P.inc(ins, EV)

    def emit_proj_fm(win_sb, win_base, xin, xin_base, dst, dst_base, nj,
                     act=None, bias_cols=None, bias_base=0, tok_w=2048, kstride=1024):
        with nc.Block() as blk:
            base_ev = P.val(EV)
            n_grp = nj * 4
            @blk.tensor
            def _(t):
                grp = 0
                for j in range(nj):
                    for ti in range(4):
                        slot = grp % 6
                        if grp >= 6:
                            t.wait_ge(P.S(EV), base_ev + grp - 5)
                        for k in range(NCH):
                            ins = t.matmul(ps[slot][:, :],
                                           ap(win_sb, win_base + k * kstride + j * 128, 128),
                                           ap(xin, xin_base + k * tok_w + ti * 512, 512),
                                           start=(k == 0), stop=(k == NCH - 1))
                        P.inc(ins, MM)
                        grp += 1
            base_mm = P.val(MM) - n_grp
            eng = "scalar" if act is not None else "vector"
            if act is None:
                @blk.vector
                def _(v):
                    grp = 0
                    for j in range(nj):
                        for ti in range(4):
                            slot = grp % 6
                            v.wait_ge(P.S(MM), base_mm + grp + 1)
                            ins = v.tensor_copy(ap(dst, dst_base + j * tok_w + ti * 512, 512),
                                                ps[slot][:, :])
                            P.inc(ins, EV)
                            grp += 1
            else:
                @blk.scalar
                def _(s):
                    grp = 0
                    for j in range(nj):
                        for ti in range(4):
                            slot = grp % 6
                            s.wait_ge(P.S(MM), base_mm + grp + 1)
                            kw = {}
                            if bias_cols is not None:
                                kw["bias"] = ap(bias_cols, bias_base + j, 1)
                            ins = s.activation(ap(dst, dst_base + j * tok_w + ti * 512, 512),
                                               ps[slot][:, :], act, **kw)
                            P.inc(ins, EV)
                            grp += 1

    def emit_tap(dram, sb, base, width):
        with nc.Block() as blk:
            @blk.gpsimd
            def _(g):
                P.inc(g.dma_start(dram[:, 0:width], ap(sb, base, width)), ST, 16)
                g.wait_ge(P.S(ST), P.val(ST))

    def emit_l0_attn():
        emit_proj_fm(Fb, 0, A, 0, Bb, 0, NCH)
        if "xd_fm" in tapd:
            emit_tap(tapd["xd_fm"], Bb, 0, NCH * TOK)
        with nc.Block() as blk:
            @blk.gpsimd
            def _(g):
                for k in range(NCH):
                    P.inc(g.dma_start(ap(Fb, k * 1024, 1024), w_uni[k * 128:(k + 1) * 128, :]), GQ, 16)
        KP, QP, VP, SQc = 0, 2048, 4096, 2048
        for pair in range(NPAIR):
            with nc.Block() as blk:
                base_ev = P.val(EV)
                @blk.tensor
                def _(t):
                    g2 = 0
                    for ti in range(4):
                        for pi in range(3):
                            slot = g2 % 6
                            if g2 >= 6:
                                t.wait_ge(P.S(EV), base_ev + g2 - 5)
                            col = TOKT + (pi * 8 + pair) * 64
                            ins = t.matmul(ps[slot][0:64, :], ap(CSB, col, 64, p0=0, pn=64),
                                           ap(Bb, pair * 2048 + ti * 512, 512, p0=0, pn=64),
                                           start=True, stop=True, tile_position=(0, 0))
                            ins = t.matmul(ps[slot][64:128, :], ap(CSB, col, 64, p0=64, pn=64),
                                           ap(Bb, pair * 2048 + ti * 512, 512, p0=64, pn=64),
                                           start=True, stop=True)
                            P.inc(ins, MM)
                            g2 += 1
                base_mm = P.val(MM) - 12
                @blk.vector
                def _(v):
                    g2 = 0
                    for ti in range(4):
                        for pi in range(3):
                            slot = g2 % 6
                            v.wait_ge(P.S(MM), base_mm + g2 + 1)
                            dstc = (KP, QP, VP)[pi] + ti * 512
                            ins = v.tensor_copy(ap(Hb, dstc, 512), ps[slot][:, :])
                            P.inc(ins, EV)
                            g2 += 1
            if pair == 0 and "kp" in tapd:
                emit_tap(tapd["kp"], Hb, KP, TOK)
                emit_tap(tapd["qp"], Hb, QP, TOK)
                emit_tap(tapd["vp"], Hb, VP, TOK)
            emit_v_transpose(Hb, VP)
            emit_negC(KP, QP, SQc, 0.0625, Hb)
            emit_attn_pair2(pair, l1=False, k_cols=KP, q_cols=QP)
        if "av_fm" in tapd:
            emit_tap(tapd["av_fm"], C, 0, NCH * TOK)
        emit_unify_blockend(Fb, 0, CSB, BUNIB, x_in, first=True)

    def emit_vt_init():
        with nc.Block() as blk:
            @blk.gpsimd
            def _(g):
                g.memset(VT[:, :], 0.0)
                for kt in range(16):
                    g.memset(ap(VT, kt * 80 + 64, 1), 1.0)

    def emit_v_transpose(v_t, VP):
        with nc.Block() as blk:
            base_mm = P.val(MM)
            @blk.tensor
            def _(t):
                for half in range(2):
                    pt = (pt0, pt1)[half]
                    for k8 in range(8):
                        kt = half * 8 + k8
                        ins = t.transpose(pt[:, k8 * 128:(k8 + 1) * 128],
                                          ap(v_t, VP + kt * 128, 128), ap(CSB, IDB, 128))
                    P.inc(ins, MM)
            @blk.vector
            def _(v):
                for half in range(2):
                    pt = (pt0, pt1)[half]
                    v.wait_ge(P.S(MM), base_mm + half + 1)
                    srcA = bass.AP(pt, 0, [[1024, 128], [128, 8], [1, 64]])
                    dstA = bass.AP(VT, half * 8 * 80, [[VT.shape[1], 128], [80, 8], [1, 64]])
                    v.tensor_copy(dstA, srcA)
                    srcB = bass.AP(pt, 64, [[1024, 128], [128, 8], [1, 64]])
                    dstB = bass.AP(VT, 1280 + half * 8 * 64, [[VT.shape[1], 128], [64, 8], [1, 64]])
                    ins = v.tensor_copy(dstB, srcB)
                    P.inc(ins, EV)

    def emit_negC(KP, QP, SQc, coef, kq_t, sq_t=None):
        sq_t = sq_t if sq_t is not None else E
        """negC for both heads -> SM[S_NC..+2]: C = coef*(max|q|^2 + max|k|^2) per head."""
        kq_list = kq_t if isinstance(kq_t, (list, tuple)) else (kq_t, kq_t)
        for which, colbase, qm_slot in ((0, QP, 4), (1, KP, 5)):
            with nc.Block() as blk:
                @blk.scalar
                def _(s):
                    ins = s.activation(ap(sq_t, SQc, 2048), ap(kq_list[which], colbase, 2048), AF.Square)
                    P.inc(ins, AC)
                base_mm = P.val(MM)
                @blk.tensor
                def _(t):
                    t.wait_ge(P.S(AC), P.val(AC))
                    for ti in range(4):
                        ins = t.matmul(ps[ti][0:2, :], ap(CSB, E2B, 2), ap(sq_t, SQc + ti * 512, 512),
                                       start=True, stop=True)
                        P.inc(ins, MM)
                @blk.vector
                def _(v):
                    v.wait_ge(P.S(MM), base_mm + 4)
                    for ti in range(4):
                        v.tensor_reduce(ap(SM, S_QM + ti, 1, pn=2), ps[ti][0:2, :], AX.X, ALU.max)
                    v.drain()
                    ins = v.tensor_reduce(ap(SM, S_QM + qm_slot, 1, pn=2), ap(SM, S_QM, 4, pn=2),
                                          AX.X, ALU.max)
                    P.inc(ins, EV)
        with nc.Block() as blk:
            base_mm = P.val(MM)
            @blk.vector
            def _(v):
                v.tensor_tensor(ap(SM, S_QM + 6, 1, pn=2), ap(SM, S_QM + 4, 1, pn=2),
                                ap(SM, S_QM + 5, 1, pn=2), ALU.add)
                v.drain()
                v.tensor_scalar(ap(SM, S_QM + 7, 1, pn=2), ap(SM, S_QM + 6, 1, pn=2),
                                coef, 0.0, ALU.mult, ALU.add)
                v.drain()
                ins = v.tensor_scalar(ap(SM, S_C2, 2, pn=2), IDF[0:2, 0:2], ap(SM, S_QM + 7, 1, pn=2),
                                      1.0, ALU.mult, ALU.mult)
                P.inc(ins, EV)
            @blk.tensor
            def _(t):
                t.wait_ge(P.S(EV), P.val(EV))
                ins = t.matmul(ps[0][:, 0:2], ap(CST, ONES2, 128, pn=2), ap(SM, S_C2, 2, pn=2),
                               start=True, stop=True, tile_position=(0, 0))
                P.inc(ins, MM)
            @blk.vector
            def _(v2):
                v2.wait_ge(P.S(MM), base_mm + 1)
                ins = v2.tensor_scalar(ap(SM, S_NC, 2), ps[0][:, 0:2], -1.0, 0.0, ALU.mult, ALU.add)
                P.inc(ins, EV)

    def emit_attn_pair2(pair, l1, k_cols, q_cols, k_t=None, q_t=None,
                        p_t=None, p_base=10240, t32=None, t32_base=1024,
                        r32=None, r32_base=7168):
        k_t = k_t if k_t is not None else Hb
        q_t = q_t if q_t is not None else Hb
        p_t = p_t if p_t is not None else E
        t32 = t32 if t32 is not None else Ef32
        r32 = r32 if r32 is not None else Ef32
        """scores+exp+AV+normalize for one pair. Writes normalized AV to C[pair*2048 + qt*512]."""
        sched = pair_sched(pair, l1)
        hA, hB = 2 * pair, 2 * pair + 1
        PBUF = p_base
        TSB = t32_base
        RSB = r32_base
        DSBA, DSBB = 0, 512
        with nc.Block() as blk:
            base_mm, base_ac, base_ev, base_dn = P.val(MM), P.val(AC), P.val(EV), P.val("dn")

            @blk.tensor
            def _(t):
                sc_i, mm_i = 0, 0
                for qt in range(NQT):
                    nk = len(sched[qt])
                    for i, ktp in enumerate(sched[qt]):
                        if sc_i >= 1:
                            if l1:
                                t.wait_ge(P.S(EV), base_ev + 2 * sc_i)
                            else:
                                t.wait_ge(P.S(AC), base_ac + 2 * sc_i)
                        for sub in range(2):
                            kt = 2 * ktp + sub
                            ins = t.matmul(ps[0 + sub][:, :], ap(k_t, k_cols + kt * 128, 128, p0=0, pn=64),
                                           ap(q_t, q_cols + qt * 512, 512, p0=0, pn=64),
                                           start=True, stop=True, tile_position=(0, 0))
                            ins = t.matmul(ps[2 + sub][:, :], ap(k_t, k_cols + kt * 128, 128, p0=64, pn=64),
                                           ap(q_t, q_cols + qt * 512, 512, p0=64, pn=64),
                                           start=True, stop=True, tile_position=(64, 0))
                        P.inc(ins, MM)
                        sc_i += 1
                        mm_i += 1
                        t.wait_ge(P.S(AC), base_ac + 2 * sc_i)
                        pb = (sc_i - 1) % 2
                        for sub in range(2):
                            st_ = (i == 0 and sub == 0)
                            sp_ = (i == nk - 1 and sub == 1)
                            ins = t.matmul(ps[4][0:65, :], ap(VT, (2 * ktp + sub) * 80, 65),
                                           ap(p_t, PBUF + pb * 1024 + sub * 512, 512),
                                           start=st_, stop=sp_, tile_position=(0, 0))
                            ins = t.matmul(ps[5][64:128, :], ap(VT, 1280 + (2 * ktp + sub) * 64, 64),
                                           ap(p_t, PBUF + 2048 + pb * 1024 + sub * 512, 512),
                                           start=st_, stop=sp_)
                            ins = t.matmul(ps[5][0:1, :], ap(VT, 64, 1),
                                           ap(p_t, PBUF + 2048 + pb * 1024 + sub * 512, 512),
                                           start=st_, stop=sp_, tile_position=(0, 0))
                        P.inc(ins, MM)
                        mm_i += 1
                    t.wait_ge(P.S("dn"), base_dn + 2 * qt + 1)
                    ins = t.matmul(ps[0][:, :], ONF[64:65, :], ap(H, DSBA, 512, p0=64, pn=1),
                                   start=True, stop=True, tile_position=(64, 0))
                    ins = t.matmul(ps[1][:, :], ONF[0:1, :], ap(H, DSBB, 512, p0=0, pn=1),
                                   start=True, stop=True, tile_position=(0, 0))
                    P.inc(ins, MM)
                    mm_i += 1
                    if qt < NQT - 1:
                        t.wait_ge(P.S("dn"), base_dn + 2 * qt + 2)

            @blk.vector
            def _(v):
                sc_i, mm_i = 0, 0
                for qt in range(NQT):
                    nk = len(sched[qt])
                    for i, ktp in enumerate(sched[qt]):
                        if l1:
                            v.wait_ge(P.S(MM), base_mm + mm_i + 1)
                            pb = sc_i % 2
                            band = 2 * ktp - 4 * qt + 12
                            for hi, h in enumerate((hA, hB)):
                                for sub in range(2):
                                    ins = v.scalar_tensor_tensor(
                                        ap(t32, TSB + (2 * hi + pb) * 1024 + sub * 512, 512),
                                        ap(DM, (band + sub) * 512, 512), -8.0 * SLOPES[h],
                                        ps[2 * hi + sub][:, :], ALU.mult, ALU.add)
                                P.inc(ins, EV)
                        sc_i += 1
                        mm_i += 2
                    mm_i += 1
                    v.wait_ge(P.S(MM), base_mm + mm_i - 1)
                    v.tensor_scalar(ap(H, DSBA, 512, p0=64, pn=1), ps[4][64:65, :], 1e-30, 0.0,
                                    ALU.add, ALU.add)
                    v.tensor_scalar(ap(H, DSBB, 512, p0=0, pn=1), ps[5][0:1, :], 1e-30, 0.0,
                                    ALU.add, ALU.add)
                    v.drain()
                    v.reciprocal(ap(H, DSBA, 512, p0=64, pn=1), ap(H, DSBA, 512, p0=64, pn=1))
                    ins = v.reciprocal(ap(H, DSBB, 512, p0=0, pn=1), ap(H, DSBB, 512, p0=0, pn=1))
                    v.drain()
                    P.inc(ins, "dn")
                    v.wait_ge(P.S(MM), base_mm + mm_i)
                    v.tensor_copy(ap(r32, RSB, 512), ps[0][:, :])
                    v.tensor_copy(ap(r32, RSB + 512, 512), ps[1][:, :])
                    v.drain()
                    v.tensor_tensor(ap(C, pair * 2048 + qt * 512, 512, p0=0, pn=64),
                                    ps[4][0:64, :], ap(r32, RSB, 512, p0=0, pn=64), ALU.mult)
                    ins = v.tensor_tensor(ap(C, pair * 2048 + qt * 512, 512, p0=64, pn=64),
                                          ps[5][64:128, :], ap(r32, RSB + 512, 512, p0=64, pn=64), ALU.mult)
                    v.drain()
                    P.inc(ins, "dn")

            @blk.scalar
            def _(s):
                sc_i, mm_i, ev_i = 0, 0, 0
                for qt in range(NQT):
                    for i, ktp in enumerate(sched[qt]):
                        pb = sc_i % 2
                        if l1:
                            ev_i += 2
                            s.wait_ge(P.S(EV), base_ev + ev_i)
                            s.activation(ap(p_t, PBUF + pb * 1024, 1024), ap(t32, TSB + pb * 1024, 1024),
                                         AF.Exp, scale=0.125, bias=ap(SM, S_NC, 1)).then_inc(P.S(AC), 1)
                            P.cnt[AC] += 1
                            ins = s.activation(ap(p_t, PBUF + 2048 + pb * 1024, 1024),
                                               ap(t32, TSB + (2 + pb) * 1024, 1024),
                                               AF.Exp, scale=0.125, bias=ap(SM, S_NC + 1, 1))
                            P.inc(ins, AC)
                        else:
                            s.wait_ge(P.S(MM), base_mm + mm_i + 1)
                            s.activation(ap(p_t, PBUF + pb * 1024, 512), ps[0][:, :], AF.Exp,
                                         scale=0.125, bias=ap(SM, S_NC, 1))
                            s.activation(ap(p_t, PBUF + pb * 1024 + 512, 512), ps[1][:, :], AF.Exp,
                                         scale=0.125, bias=ap(SM, S_NC, 1)).then_inc(P.S(AC), 1)
                            P.cnt[AC] += 1
                            s.activation(ap(p_t, PBUF + 2048 + pb * 1024, 512), ps[2][:, :], AF.Exp,
                                         scale=0.125, bias=ap(SM, S_NC + 1, 1))
                            ins = s.activation(ap(p_t, PBUF + 2048 + pb * 1024 + 512, 512), ps[3][:, :],
                                               AF.Exp, scale=0.125, bias=ap(SM, S_NC + 1, 1))
                            P.inc(ins, AC)
                        sc_i += 1
                        mm_i += 2
                    mm_i += 1

    def emit_blockend_stats(ytile):
        with nc.Block() as blk:
            @blk.scalar
            def _(s):
                for ti in range(NT):
                    ins = s.activation(ap(H, 0, 1024), ytile(ti), AF.Square,
                                       accum_out=ap(SM, S_SQ + ti, 1))
                    P.inc(ins, AC)
            @blk.vector
            def _(v):
                for ti in range(NT):
                    ins = v.reduce_sum(ap(SM, S_SUM + ti, 1), ytile(ti), AX.X)
                P.inc(ins, EV)
        with nc.Block() as blk:
            @blk.vector
            def _(v):
                v.tensor_scalar(ap(SM, S_MEAN, 16), ap(SM, S_SUM, 16), 1.0 / D, 0.0, ALU.mult, ALU.add)
                v.drain()
                v.tensor_tensor(ap(SM, S_TMP, 16), ap(SM, S_MEAN, 16), ap(SM, S_MEAN, 16), ALU.mult)
                v.drain()
                v.scalar_tensor_tensor(ap(SM, S_VAR, 16), ap(SM, S_SQ, 16), 1.0 / D, ap(SM, S_TMP, 16),
                                       ALU.mult, ALU.subtract)
                v.drain()
                v.tensor_scalar(ap(SM, S_VAR, 16), ap(SM, S_VAR, 16), EPS, 0.0, ALU.add, ALU.add)
                v.drain()
                ins = v.reciprocal(ap(SM, S_TMP, 16), ap(SM, S_VAR, 16))
                P.inc(ins, EV)
            @blk.scalar
            def _(s):
                s.wait_ge(P.S(EV), P.val(EV))
                ins = s.activation(ap(SM, S_R2, 16), ap(SM, S_TMP, 16), AF.Sqrt)
                P.inc(ins, AC)
        with nc.Block() as blk:
            @blk.vector
            def _(v):
                ins = v.scalar_tensor_tensor(ap(SM, S_NMR2, 16), ap(SM, S_MEAN, 16), -1.0,
                                             ap(SM, S_R2, 16), ALU.mult, ALU.mult)
                P.inc(ins, EV)

    def emit_unify_blockend(wsb, wbase, bias_sb, bias_base, x_src, first=False, out_dst=None):
        with nc.Block() as blk:
            base_mm, base_ev = P.val(MM), P.val(EV)
            @blk.tensor
            def _(t):
                for ti in range(NT):
                    for dt_ in range(2):
                        grp = 2 * ti + dt_
                        slot = grp % 6
                        if grp >= 6:
                            t.wait_ge(P.S(EV), base_ev + grp - 5)
                        for k in range(NCH):
                            ins = t.matmul(ps[slot][:, :],
                                           ap(C, k * 2048 + ti * 128, 128),
                                           ap(wsb, wbase + k * 1024 + dt_ * 512, 512),
                                           start=(k == 0), stop=(k == NCH - 1))
                        P.inc(ins, MM)
            @blk.vector
            def _(v):
                for ti in range(NT):
                    for dt_ in range(2):
                        grp = 2 * ti + dt_
                        slot = grp % 6
                        v.wait_ge(P.S(MM), base_mm + grp + 1)
                        ins = v.tensor_tensor(ap(G, ti * 1024 + dt_ * 512, 512), ps[slot][:, :],
                                              ap(bias_sb, bias_base + dt_ * 512, 512), ALU.add)
                        P.inc(ins, EV)
            base_ev2 = P.val(EV) - 2 * NT
            @blk.scalar
            def _(s):
                for ti in range(NT):
                    s.wait_ge(P.S(EV), base_ev2 + 2 * ti + 2)
                    ins = s.activation(ap(H, 0, 1024), ap(G, ti * 1024, 1024), AF.Square,
                                       accum_out=ap(SM, S_SQ + ti, 1))
                    P.inc(ins, AC)
            @blk.vector
            def _(v):
                for ti in range(NT):
                    v.wait_ge(P.S(EV), base_ev2 + 2 * ti + 2)
                    ins = v.reduce_sum(ap(SM, S_SUM + ti, 1), ap(G, ti * 1024, 1024), AX.X)
                P.inc(ins, EV)
        with nc.Block() as blk:
            @blk.vector
            def _(v):
                v.wait_ge(P.S(AC), P.val(AC))
                v.tensor_scalar(ap(SM, S_MEAN, 16), ap(SM, S_SUM, 16), 1.0 / D, 0.0, ALU.mult, ALU.add)
                v.drain()
                v.tensor_tensor(ap(SM, S_TMP, 16), ap(SM, S_MEAN, 16), ap(SM, S_MEAN, 16), ALU.mult)
                v.drain()
                v.scalar_tensor_tensor(ap(SM, S_VAR, 16), ap(SM, S_SQ, 16), 1.0 / D, ap(SM, S_TMP, 16),
                                       ALU.mult, ALU.subtract)
                v.drain()
                v.tensor_scalar(ap(SM, S_VAR, 16), ap(SM, S_VAR, 16), EPS, 0.0, ALU.add, ALU.add)
                v.drain()
                ins = v.reciprocal(ap(SM, S_TMP, 16), ap(SM, S_VAR, 16))
                P.inc(ins, EV)
            @blk.scalar
            def _(s):
                s.wait_ge(P.S(EV), P.val(EV))
                ins = s.activation(ap(SM, S_R2, 16), ap(SM, S_TMP, 16), AF.Sqrt)
                P.inc(ins, AC)
        with nc.Block() as blk:
            @blk.vector
            def _(v):
                ins = v.scalar_tensor_tensor(ap(SM, S_NMR2, 16), ap(SM, S_MEAN, 16), -1.0,
                                             ap(SM, S_R2, 16), ALU.mult, ALU.mult)
                P.inc(ins, EV)
        emit_blockend_pass2(lambda ti: ap(G, ti * 1024, 1024), x_src, out_dst,
                            bufs=(Ef32, 1024))

    def emit_blockend_pass2(ytile, x_src, out_dst, bufs):
        buf_t, buf_c = bufs
        dst_dram = out_d if out_dst == "out" else x_state
        with nc.Block() as blk:
            base_sy = P.val(SY)
            base_ev = P.val(EV)
            @blk.sync
            def _(sy):
                for ti in range(NT):
                    if ti >= 2:
                        sy.wait_ge(P.S(EV), base_ev + ti - 1)
                    P.inc(sy.dma_start(ap(buf_t, buf_c + (ti % 2) * 1024, 1024),
                                       x_src[ti * 128:(ti + 1) * 128, :]), SY, 16)
            @blk.scalar
            def _(s):
                for ti in range(NT):
                    if ti >= 2:
                        s.wait_ge(P.S(EV), base_ev + ti - 1)
                    ins = s.activation(ap(buf_t, buf_c + 2048 + (ti % 2) * 1024, 1024), ytile(ti),
                                       AF.Identity, bias=ap(SM, S_NMR2 + ti, 1),
                                       scale=ap(SM, S_R2 + ti, 1))
                    P.inc(ins, AC)
            base_ac = P.val(AC) - NT
            @blk.vector
            def _(v):
                for ti in range(NT):
                    v.wait_ge(P.S(AC), base_ac + ti + 1)
                    v.wait_ge(P.S(SY), base_sy + 16 * (ti + 1))
                    ins = v.tensor_tensor(ytile(ti), ap(buf_t, buf_c + 2048 + (ti % 2) * 1024, 1024),
                                          ap(buf_t, buf_c + (ti % 2) * 1024, 1024), ALU.add)
                    P.inc(ins, EV)
            @blk.gpsimd
            def _(g):
                for ti in range(NT):
                    g.wait_ge(P.S(EV), base_ev + ti + 1)
                    P.inc(g.dma_start(dst_dram[ti * 128:(ti + 1) * 128, :], ytile(ti)), ST, 16)
                g.wait_ge(P.S(ST), P.val(ST))


    def emit_l0_ffn(out_dst=None):
        emit_ln(x_state)
        emit_transpose_tm_fm(C, 0, 1024, A, 0, TOK, NT, NCH)
        WG1C = 4096
        with nc.Block() as blk:
            @blk.gpsimd
            def _(g):
                for kk in range(NCH):
                    P.inc(g.dma_start(ap(Hb, WG1C + kk * 256, 256), wg1[kk * 128:(kk + 1) * 128, :]), GQ, 16)
                g.wait_ge(P.S(GQ), P.val(GQ))
        emit_proj_fm(Hb, WG1C, A, 0, Hb, 0, 2, act=AF.Gelu, bias_cols=CST, bias_base=BG1C,
                     kstride=256)
        with nc.Block() as blk:
            base_mm, base_ev, base_ac0 = P.val(MM), P.val(EV), P.val(AC)
            @blk.tensor
            def _(t):
                for ti in range(NT):
                    slot = ti % 2
                    if ti >= 2:
                        t.wait_ge(P.S(AC), base_ac0 + ti - 1)
                    for kk in range(2):
                        ins = t.matmul(ps[slot][:, 0:8], ap(Hb, kk * 2048 + ti * 128, 128),
                                       ap(CSB, WG2B + kk * 8, 8), start=(kk == 0), stop=(kk == 1))
                    P.inc(ins, MM)
            @blk.vector
            def _(v):
                for ti in range(NT):
                    slot = ti % 2
                    v.wait_ge(P.S(MM), base_mm + ti + 1)
                    v.tensor_reduce(ap(SM, 355, 1), ps[slot][:, 0:8], AX.X, ALU.max)
                    v.drain()
                    ins = v.tensor_scalar(ap(SM, 356, 1), ap(SM, 355, 1), -1.0, 0.0, ALU.mult, ALU.add)
                    v.drain()
                    P.inc(ins, EV)
                    v.wait_ge(P.S(AC), base_ac0 + ti + 1)
                    v.tensor_reduce(ap(SM, 353, 1), ap(SM, 336, 8), AX.X, ALU.add)
                    v.drain()
                    v.reciprocal(ap(SM, 354, 1), ap(SM, 353, 1))
                    v.drain()
                    v.tensor_scalar(ap(SM, 336, 8), ap(SM, 336, 8), ap(SM, 354, 1), 1.0,
                                    ALU.mult, ALU.mult)
                    v.drain()
                    v.max(ap(SM, 344, 8), ap(SM, 336, 8))
                    v.drain()
                    v.tensor_scalar(ap(SM, 352, 1), ap(SM, 345, 1), 1.0, 0.0, ALU.mult, ALU.add)
                    v.drain()
                    v.tensor_scalar(ap(SM, 360, 8), ap(SM, 336, 8), ap(SM, 352, 1), 1.0,
                                    ALU.is_ge, ALU.mult)
                    v.drain()
                    ins = v.tensor_tensor(ap(SM, S_G + ti * 8, 8), ap(SM, 336, 8), ap(SM, 360, 8),
                                          ALU.mult)
                    P.inc(ins, EV)
            @blk.scalar
            def _(s):
                for ti in range(NT):
                    s.wait_ge(P.S(EV), base_ev + 2 * ti + 1)
                    slot = ti % 2
                    ins = s.activation(ap(SM, 336, 8), ps[slot][:, 0:8], AF.Exp,
                                       scale=1.0, bias=ap(SM, 356, 1))
                    P.inc(ins, AC)
        if "gates" in tapd:
            emit_tap(tapd["gates"], SM, S_G, NT * 8)

        WQ0, WQ1 = 0, 4096
        def woslot(j):
            return (Hb, j * 1024) if j < 6 else (VT, (j - 6) * 1024)
        WI, WO = "wi", "wo"
        with nc.Block() as blk:
            base_mm_e, base_ac_e, base_ev_e = P.val(MM), P.val(AC), P.val(EV)
            base_wi, base_wo = P.val(WI), P.val(WO)
            def h_done(e, q):
                return base_mm_e + e * 80 + (q + 1) * 4
            def o_done(e, half):
                return base_mm_e + e * 80 + 16 + (half + 1) * 32

            @blk.gpsimd
            def _(g):
                def win(e, q):
                    fbase = (WQ0, WQ1)[(e * 4 + q) % 2]
                    for kk in range(NCH):
                        P.inc(g.dma_start(ap(Fb, fbase + kk * 512, 512),
                                          bass.AP(w_ine, e * D * 2048 + kk * 128 * 2048 + q * 512,
                                                  [[2048, 128], [1, 512]])), WI, 16)
                def wout(e, half):
                    for j in range(8):
                        hh = half * 8 + j
                        tt_, cc = woslot(j)
                        P.inc(g.dma_start(ap(tt_, cc, 1024),
                                          bass.AP(w_oute, (e * 2048 + hh * 128) * D,
                                                  [[D, 128], [1, 1024]])),
                              WO, 16)
                win(0, 0); win(0, 1)
                for e in range(8):
                    g.wait_ge(P.S(MM), h_done(e, 0))
                    win(e, 2)
                    g.wait_ge(P.S(MM), h_done(e, 1))
                    win(e, 3)
                    if e > 0:
                        g.wait_ge(P.S(MM), o_done(e - 1, 1))
                    wout(e, 0)
                    if e < 7:
                        g.wait_ge(P.S(MM), h_done(e, 2))
                        win(e + 1, 0)
                    g.wait_ge(P.S(MM), o_done(e, 0))
                    wout(e, 1)
                    if e < 7:
                        g.wait_ge(P.S(MM), h_done(e, 3))
                        win(e + 1, 1)

            @blk.tensor
            def _(t):
                for e in range(8):
                    for q in range(4):
                        gq = e * 4 + q
                        fbase = (WQ0, WQ1)[gq % 2]
                        t.wait_ge(P.S(WI), base_wi + (gq + 1) * 128)
                        for hq in range(4):
                            for tt_ in range(4):
                                gidx = gq * 16 + hq * 4 + tt_
                                gslot = gidx % 6
                                if gidx >= 6:
                                    t.wait_ge(P.S(AC), base_ac_e + gidx - 5)
                                for kk in range(NCH):
                                    ins = t.matmul(ps[gslot][:, :],
                                                   ap(Fb, fbase + kk * 512 + hq * 128, 128),
                                                   ap(A, kk * 2048 + tt_ * 512, 512),
                                                   start=(kk == 0), stop=(kk == NCH - 1))
                            ins.then_inc(P.S(MM), 1)
                            P.cnt[MM] += 1
                    t.wait_ge(P.S(AC), base_ac_e + (e + 1) * 64)
                    for half in range(2):
                        t.wait_ge(P.S(WO), base_wo + (e * 2 + half + 1) * 128)
                        for ti in range(NT):
                            for dt_ in range(2):
                                gidx = half * 32 + ti * 2 + dt_
                                gslot = gidx % 6
                                if e * 64 + gidx >= 6:
                                    t.wait_ge(P.S(EV), base_ev_e + e * 64 + gidx - 5)
                                for j in range(8):
                                    hh = half * 8 + j
                                    wt, wc = woslot(j)
                                    harena, hcol = (C, hh * 2048) if hh < 8 else (Dd, (hh - 8) * 2048)
                                    ins = t.matmul(ps[gslot][:, :],
                                                   ap(harena, hcol + ti * 128, 128),
                                                   ap(wt, wc + dt_ * 512, 512),
                                                   start=(j == 0), stop=(j == 7))
                                ins.then_inc(P.S(MM), 1)
                                P.cnt[MM] += 1

            @blk.scalar
            def _(s):
                for e in range(8):
                    if e > 0:
                        s.wait_ge(P.S(MM), o_done(e - 1, 1))
                    for q in range(4):
                        gq = e * 4 + q
                        for hq in range(4):
                            H_ = q * 4 + hq
                            s.wait_ge(P.S(MM), base_mm_e + e * 80 + q * 4 + hq + 1)
                            for tt_ in range(4):
                                gidx = gq * 16 + hq * 4 + tt_
                                gslot = gidx % 6
                                harena, hcol = (C, H_ * 2048) if H_ < 8 else (Dd, (H_ - 8) * 2048)
                                ins = s.activation(ap(harena, hcol + tt_ * 512, 512), ps[gslot][:, :],
                                                   AF.Gelu)
                                ins.then_inc(P.S(AC), 1)
                                P.cnt[AC] += 1

            @blk.vector
            def _(v):
                for e in range(8):
                    for half in range(2):
                        for ti in range(NT):
                            for dt_ in range(2):
                                gidx = half * 32 + ti * 2 + dt_
                                gslot = gidx % 6
                                v.wait_ge(P.S(MM), base_mm_e + e * 80 + 16 + gidx + 1)
                                yt, yc = (YS0, ti * 1024) if ti < 8 else (YS1, (ti - 8) * 1024)
                                gate = ap(SM, S_G + ti * 8 + e, 1)
                                if e == 0 and half == 0:
                                    ins = v.tensor_scalar(ap(yt, yc + dt_ * 512, 512), ps[gslot][:, :],
                                                          gate, 0.0, ALU.mult, ALU.add)
                                else:
                                    ins = v.scalar_tensor_tensor(ap(yt, yc + dt_ * 512, 512),
                                                                 ps[gslot][:, :], gate,
                                                                 ap(yt, yc + dt_ * 512, 512),
                                                                 ALU.mult, ALU.add)
                                ins.then_inc(P.S(EV), 1)
                                P.cnt[EV] += 1
        def ytile_l0f(ti):
            return ap(YS0, ti * 1024, 1024) if ti < 8 else ap(YS1, (ti - 8) * 1024, 1024)
        emit_blockend_stats(ytile_l0f)
        emit_blockend_pass2(ytile_l0f, x_state, out_dst, bufs=(G, 0))


    def emit_l1_attn(out_dst=None):
        emit_ln(x_state)
        emit_transpose_tm_fm(C, 0, 1024, A, 0, TOK, NT, NCH)
        with nc.Block() as blk:
            @blk.sync
            def _(sy):
                P.inc(sy.dma_start(ap(H, 0, 1024, pn=1), bout.ap()[None, :]), SY, 16)
        with nc.Block() as blk:
            @blk.tensor
            def _(t):
                t.wait_ge(P.S(SY), P.val(SY))
                for dt_ in range(2):
                    ins = t.matmul(ps[dt_][:, :], ONF[0:1, :], ap(H, dt_ * 512, 512, pn=1),
                                   start=True, stop=True, tile_position=(0, 0))
                P.inc(ins, MM)
            @blk.vector
            def _(v):
                v.wait_ge(P.S(MM), P.val(MM))
                for dt_ in range(2):
                    v.tensor_copy(ap(CSB, BUNIB + dt_ * 512, 512), ps[dt_][:, :])
        WV = "wv"
        base_wv = P.val(WV)
        with nc.Block() as blk:
            base_ev = P.val(EV)
            @blk.gpsimd
            def _(g):
                for pair in range(NPAIR):
                    if pair >= 2:
                        g.wait_ge(P.S(EV), base_ev + (pair - 1) * 12)
                    fbase = (pair % 2) * 3072
                    for pi in range(3):
                        src_ap = bass.AP(wqkv, pi * 1024 + pair * 128,
                                         [[3072, 128], [128 * 3072, 8], [1, 128]])
                        P.inc(g.dma_start(ap(Fb, fbase + pi * 1024, 1024), src_ap), WV, 16)
            @blk.tensor
            def _(t):
                for pair in range(NPAIR):
                    fbase = (pair % 2) * 3072
                    t.wait_ge(P.S(WV), base_wv + (pair + 1) * 48)
                    for pi in range(3):
                        for ti in range(4):
                            grp = pair * 12 + pi * 4 + ti
                            slot = grp % 6
                            if grp >= 6:
                                t.wait_ge(P.S(EV), base_ev + grp - 5)
                            for kk in range(NCH):
                                ins = t.matmul(ps[slot][:, :],
                                               ap(Fb, fbase + pi * 1024 + kk * 128, 128),
                                               ap(A, kk * 2048 + ti * 512, 512),
                                               start=(kk == 0), stop=(kk == NCH - 1))
                            P.inc(ins, MM)
            base_mm2 = P.val(MM) - NPAIR * 12
            @blk.scalar
            def _(s):
                for pair in range(NPAIR):
                    for pi in range(3):
                        dst_t = (Bb, Dd, E)[pi] if True else None
                        for ti in range(4):
                            grp = pair * 12 + pi * 4 + ti
                            slot = grp % 6
                            s.wait_ge(P.S(MM), base_mm2 + grp + 1)
                            ins = s.activation(ap(dst_t, pair * 2048 + ti * 512, 512), ps[slot][:, :],
                                               AF.Identity, bias=ap(CST, BQKV + pi * 8 + pair, 1))
                            ins.then_inc(P.S(EV), 1)
                            P.cnt[EV] += 1
        with nc.Block() as blk:
            @blk.sync
            def _(sy):
                P.inc(sy.dma_start(DM[:, :], dmast[:, :]), SY, 16)
                sy.wait_ge(P.S(SY), P.val(SY))
        emit_vt_init()
        for pair in range(NPAIR):
            emit_v_transpose(E, pair * 2048)
            emit_negC(pair * 2048, pair * 2048, 4096, 0.0625, (Bb, Dd), sq_t=Hb)
            if pair == 0 and "sm1" in tapd:
                emit_tap(tapd["sm1"], SM, 0, 384)
            emit_attn_pair2(pair, l1=True, k_cols=pair * 2048, q_cols=pair * 2048, k_t=Dd, q_t=Bb,
                            p_t=Hb, p_base=0, t32=Ff32, t32_base=0, r32=G, r32_base=7168)
            if pair == 0 and "t_dbg" in tapd:
                with nc.Block() as blk2:
                    @blk2.gpsimd
                    def _(g):
                        P.inc(g.dma_start(tapd["t_dbg"][:, :],
                                          bass.AP(Ff32, 0, [[4096, 128], [1, 4096]])), ST, 16)
                        g.wait_ge(P.S(ST), P.val(ST))
        if "av_fm" in tapd:
            emit_tap(tapd["av_fm"], C, 0, NCH * TOK)
        with nc.Block() as blk:
            @blk.gpsimd
            def _(g):
                for kk in range(NCH):
                    P.inc(g.dma_start(ap(Fb, kk * 1024, 1024), wout_d[kk * 128:(kk + 1) * 128, :]), GQ, 16)
                g.wait_ge(P.S(GQ), P.val(GQ))
        emit_unify_blockend(Fb, 0, CSB, BUNIB, x_state, out_dst=out_dst)


    def hs_tile(ti):
        arena = (C, Dd, E, Bb)[ti // 4]
        return arena, (ti % 4) * 4096

    def hs_f32(ti):
        arena = (CF32, DF32, YS1, YS0)[ti // 4]
        return arena, (ti % 4) * 2048

    def emit_wstats(wdram, nrow_chunks, row_w, dst_scol):
        nhalf = max(1, row_w // 2048)
        cw_ = min(row_w, 2048)
        n = nrow_chunks * nhalf
        with nc.Block() as blk:
            base_sy, base_ev = P.val(SY), P.val(EV)
            @blk.sync
            def _(sy):
                for i in range(n):
                    if i >= 2:
                        sy.wait_ge(P.S(EV), base_ev + i - 1)
                    ch, hf = divmod(i, nhalf)
                    P.inc(sy.dma_start(ap(Ff32, (i % 2) * 2048, cw_),
                                       bass.AP(wdram, ch * 128 * row_w + hf * 2048,
                                               [[row_w, 128], [1, cw_]])), SY, 16)
            @blk.vector
            def _(v):
                for i in range(n):
                    v.wait_ge(P.S(SY), base_sy + 16 * (i + 1))
                    v.tensor_reduce(ap(SM, S_WS + 5, 1), ap(Ff32, (i % 2) * 2048, cw_), AX.X,
                                    ALU.add, apply_absolute_value=True)
                    v.drain()
                    if i == 0:
                        ins = v.tensor_copy(ap(SM, S_WS + 4, 1), ap(SM, S_WS + 5, 1))
                    else:
                        ins = v.tensor_tensor(ap(SM, S_WS + 4, 1), ap(SM, S_WS + 4, 1),
                                              ap(SM, S_WS + 5, 1), ALU.add)
                    v.drain()
                    P.inc(ins, EV)
        with nc.Block() as blk:
            base_mm, base_ev = P.val(MM), P.val(EV)
            @blk.tensor
            def _(t):
                ins = t.matmul(ps[0][0:1, 0:1], ap(SM, S_WS + 4, 1), ONF[:, 0:1],
                               start=True, stop=True)
                P.inc(ins, MM)
                t.wait_ge(P.S(EV), base_ev + 1)
                t.matmul(ps[1][:, 0:1], ONF[0:1, :], ap(SM, S_WS + 6, 1, pn=1),
                         start=True, stop=True, tile_position=(0, 0))
                ins = t.matmul(ps[2][:, 0:1], ONF[0:1, :], ap(SM, S_WS + 5, 1, pn=1),
                               start=True, stop=True, tile_position=(0, 0))
                P.inc(ins, MM)
            @blk.vector
            def _(v):
                v.wait_ge(P.S(MM), base_mm + 1)
                nelem = float(nrow_chunks * 128 * row_w)
                v.tensor_scalar(ap(SM, S_WS + 5, 1, pn=1), ps[0][0:1, 0:1], 1.0 / nelem, 1e-5,
                                ALU.mult, ALU.max)
                v.drain()
                ins = v.reciprocal(ap(SM, S_WS + 6, 1, pn=1), ap(SM, S_WS + 5, 1, pn=1))
                v.drain()
                P.inc(ins, EV)
                v.wait_ge(P.S(MM), base_mm + 2)
                v.tensor_copy(ap(SM, dst_scol, 1), ps[1][:, 0:1])
                ins = v.tensor_copy(ap(SM, dst_scol + 1, 1), ps[2][:, 0:1])
                P.inc(ins, EV)

    def emit_w2_prep():
        for r in range(8):
            with nc.Block() as blk:
                base_sy, base_ev, base_mm = P.val(SY), P.val(EV), P.val(MM)
                @blk.sync
                def _(sy):
                    for hf in range(2):
                        P.inc(sy.dma_start(ap(Ff32, hf * 2048, 2048),
                                           bass.AP(w2d, r * 128 * 4096 + hf * 2048,
                                                   [[4096, 128], [1, 2048]])), SY, 16)
                @blk.vector
                def _(v):
                    for quarter in range(4):
                        v.wait_ge(P.S(SY), base_sy + 16 * (quarter // 2 + 1))
                        v.tensor_scalar(ap(H, 0, 1024), ap(Ff32, quarter * 1024, 1024),
                                        ap(SM, S_WS + 2, 1), MAGIC, ALU.mult, ALU.add)
                        v.drain()
                        v.tensor_scalar(ap(H, 0, 1024), ap(H, 0, 1024), MAGIC, 1.0,
                                        ALU.subtract, ALU.min)
                        v.drain()
                        ins = v.tensor_scalar(ap(Hb, quarter * 1024, 1024), ap(H, 0, 1024), -1.0, 0.0,
                                              ALU.max, ALU.add)
                        v.drain()
                        P.inc(ins, EV)
                @blk.tensor
                def _(t):
                    for b in range(4):
                        t.wait_ge(P.S(EV), base_ev + 2 * b // 2 + 1)
                        pt = (pt0, pt1)[b % 2]
                        if b >= 2:
                            t.wait_ge(P.S(EV), base_ev + 4 + (b - 2) + 1)
                        for s8 in range(8):
                            ss = b * 8 + s8
                            ins = t.transpose(pt[:, s8 * 128:(s8 + 1) * 128],
                                              ap(Hb, ss * 128, 128), ap(CSB, IDB, 128))
                        ins.then_inc(P.S(MM), 1)
                        P.cnt[MM] += 1
                @blk.vector
                def _(v2):
                    base_st = P.val(ST)
                    for b in range(4):
                        v2.wait_ge(P.S(MM), base_mm + b + 1)
                        if b >= 2:
                            v2.wait_ge(P.S(ST), base_st + (b - 1) * 16)
                        pt = (pt0, pt1)[b % 2]
                        ins = v2.tensor_copy(ap(Hb, 4096 + (b % 2) * 1024, 1024), pt[:, :])
                        P.inc(ins, EV)
                @blk.gpsimd
                def _(g):
                    for b in range(4):
                        g.wait_ge(P.S(EV), base_ev + 4 + b + 1)
                        dst = bass.AP(w2t_spill, (b * 8 * 128) * 1024 + r * 128,
                                      [[1024, 128], [128 * 1024, 8], [1, 128]])
                        srcs = bass.AP(Hb, 4096 + (b % 2) * 1024,
                                       [[Hb.shape[1], 128], [128, 8], [1, 128]])
                        P.inc(g.dma_start(dst, srcs), ST, 16)
                    g.wait_ge(P.S(ST), P.val(ST))

    def emit_l1_ffn(out_dst="out"):
        emit_ln(x_state, quant=True)
        emit_transpose_tm_fm(C, 0, 1024, A, 0, TOK, NT, NCH)
        if "xqA" in tapd:
            emit_tap(tapd["xqA"], A, 0, 16384)
        emit_wstats(w1d, 32, 1024, S_WS + 0)
        emit_wstats(w2d, 8, 4096, S_WS + 2)
        if "smw" in tapd:
            emit_tap(tapd["smw"], SM, 0, 384)
        emit_w2_prep()
        if "hbq" in tapd:
            emit_tap(tapd["hbq"], Hb, 0, 6144)
        if "ff" in tapd:
            emit_tap(tapd["ff"], Ff32, 0, 4096)
        HRW, SIL, WQB = 0, 2050, 4098
        WT = 4096
        with nc.Block() as blk:
            @blk.gpsimd
            def _(g):
                g.memset(ap(Hb, HRW, 1), 0.0)
                g.memset(ap(Hb, HRW + 2049, 1), 0.0)
            @blk.sync
            def _(sy):
                P.inc(sy.dma_start(ap(Ff32, 0, 1024),
                                   bass.AP(w1d, 0, [[1024, 128], [1, 1024]])), SY, 16)
        with nc.Block() as blk:
            base_sy, base_ev, base_mm, base_ac = P.val(SY), P.val(EV), P.val(MM), P.val(AC)
            def evq(hc):   return base_ev + hc * 6 + 1
            def evw(hc):   return base_ev + hc * 6 + 2
            def evhs(hc):  return base_ev + hc * 6 + 6
            def mmw(hc):   return base_mm + hc * 7 + 1
            def mmh(hc, t): return base_mm + hc * 7 + 2 + t
            def mms(hc, b): return base_mm + hc * 7 + 6 + b
            def ach(hc, t): return base_ac + hc * 8 + 1 + t
            def acs(hc, q): return base_ac + hc * 8 + 5 + q
            @blk.sync
            def _(sy):
                for hc in range(32):
                    if hc + 1 < 32:
                        if hc >= 1:
                            sy.wait_ge(P.S(EV), evq(hc - 1))
                        P.inc(sy.dma_start(ap(Ff32, ((hc + 1) % 2) * 1024, 1024),
                                           bass.AP(w1d, (hc + 1) * 128 * 1024,
                                                   [[1024, 128], [1, 1024]])), SY, 16)
            @blk.vector
            def _(v):
                for hc in range(32):
                    v.wait_ge(P.S(SY), base_sy + 16 * hc)
                    if hc >= 1:
                        v.wait_ge(P.S(MM), mmw(hc - 1))
                        v.wait_ge(P.S(AC), acs(hc - 1, 3))
                    v.tensor_scalar(ap(H, 0, 1024), ap(Ff32, (hc % 2) * 1024, 1024),
                                    ap(SM, S_WS + 0, 1), MAGIC, ALU.mult, ALU.add)
                    v.drain()
                    v.tensor_scalar(ap(H, 0, 1024), ap(H, 0, 1024), MAGIC, 1.0,
                                    ALU.subtract, ALU.min)
                    v.drain()
                    ins = v.tensor_scalar(ap(Hb, WQB, 1024), ap(H, 0, 1024), -1.0, 0.0,
                                          ALU.max, ALU.add)
                    v.drain()
                    P.inc(ins, EV)
                    v.wait_ge(P.S(MM), mmw(hc))
                    pt = (pt0, pt1)[(hc * 3) % 2]
                    ins = v.tensor_copy(ap(Fb, WT + (hc % 2) * 1024, 1024), pt[:, 0:1024])
                    P.inc(ins, EV)
                    for sub in range(4):
                        v.wait_ge(P.S(AC), ach(hc, min(sub + 1, 3)))
                        if sub >= 1:
                            v.wait_ge(P.S(AC), acs(hc, sub - 1))
                        elif hc >= 1:
                            v.wait_ge(P.S(AC), acs(hc - 1, 3))
                        v.tensor_scalar(ap(H, 0, 512), ap(Hb, HRW + sub * 512, 512),
                                        ap(CST, CONVW + hc * 3, 1), 0.0, ALU.mult, ALU.add)
                        v.drain()
                        v.scalar_tensor_tensor(ap(H, 0, 512), ap(Hb, HRW + 1 + sub * 512, 512),
                                               ap(CST, CONVW + hc * 3 + 1, 1), ap(H, 0, 512),
                                               ALU.mult, ALU.add)
                        v.drain()
                        ins = v.scalar_tensor_tensor(ap(H, 0, 512), ap(Hb, HRW + 2 + sub * 512, 512),
                                                     ap(CST, CONVW + hc * 3 + 2, 1), ap(H, 0, 512),
                                                     ALU.mult, ALU.add)
                        v.drain()
                        ins.then_inc(P.S("cv"), 1)
                        P.cnt["cv"] += 1
                    for b in range(2):
                        v.wait_ge(P.S(MM), mms(hc, b))
                        pt = (pt0, pt1)[(hc * 3 + 1 + b) % 2]
                        for half in range(2):
                            ti0 = b * 8 + half * 4
                            arena, colb = hs_tile(ti0)
                            dst = bass.AP(arena, colb + hc * 128,
                                          [[arena.shape[1], 128], [4096, 4], [1, 128]])
                            srcp = bass.AP(pt, half * 512, [[1024, 128], [128, 4], [1, 128]])
                            ins = v.tensor_copy(dst, srcp)
                            P.inc(ins, EV)
            @blk.tensor
            def _(t):
                P.S("cv")
                base_cv = 0
                for hc in range(32):
                    t.wait_ge(P.S(EV), evq(hc))
                    pt = (pt0, pt1)[(hc * 3) % 2]
                    if hc >= 1:
                        t.wait_ge(P.S(EV), evw(hc - 1) + 4)
                    for k in range(8):
                        ins = t.transpose(pt[:, k * 128:(k + 1) * 128], ap(Hb, WQB + k * 128, 128),
                                          ap(CSB, IDB, 128))
                    ins.then_inc(P.S(MM), 1)
                    P.cnt[MM] += 1
                    t.wait_ge(P.S(EV), evw(hc))
                    for tt_ in range(4):
                        slot = (hc * 4 + tt_) % 6
                        if hc * 4 + tt_ >= 6:
                            prev = hc * 4 + tt_ - 6
                            t.wait_ge(P.S(AC), ach(prev // 4, prev % 4))
                        for k in range(8):
                            ins = t.matmul(ps[slot][:, :], ap(Fb, WT + (hc % 2) * 1024 + k * 128, 128),
                                           ap(A, k * 2048 + tt_ * 512, 512),
                                           start=(k == 0), stop=(k == 7))
                        ins.then_inc(P.S(MM), 1)
                        P.cnt[MM] += 1
                    for b in range(2):
                        pt = (pt0, pt1)[(hc * 3 + 1 + b) % 2]
                        t.wait_ge(P.S(AC), acs(hc, 2 * b + 1))
                        if b == 0:
                            if hc >= 1:
                                t.wait_ge(P.S(EV), base_ev + 6 * hc)
                        else:
                            t.wait_ge(P.S(EV), base_ev + 6 * hc + 2)
                        for s8 in range(8):
                            ti = b * 8 + s8
                            ins = t.transpose(pt[:, s8 * 128:(s8 + 1) * 128],
                                              ap(Hb, SIL + ti * 128, 128), ap(CSB, IDB, 128))
                        ins.then_inc(P.S(MM), 1)
                        P.cnt[MM] += 1
            @blk.scalar
            def _(s):
                for hc in range(32):
                    for tt_ in range(4):
                        slot = (hc * 4 + tt_) % 6
                        if tt_ == 0 and hc >= 1:
                            s.wait_ge(P.S("cv"), 4 * hc)
                        s.wait_ge(P.S(MM), mmh(hc, tt_))
                        ins = s.activation(ap(Hb, HRW + 1 + tt_ * 512, 512), ps[slot][:, :],
                                           AF.Identity, bias=0.0, scale=ap(SM, S_WS + 1, 1))
                        ins.then_inc(P.S(AC), 1)
                        P.cnt[AC] += 1
                    for sub in range(4):
                        s.wait_ge(P.S("cv"), hc * 4 + sub + 1)
                        ins = s.activation(ap(Hb, SIL + sub * 512, 512), ap(H, 0, 512), AF.Silu,
                                           bias=ap(CST, CONVB + hc, 1))
                        ins.then_inc(P.S(AC), 1)
                        P.cnt[AC] += 1
        if "hsC" in tapd:
            emit_tap(tapd["hsC"], C, 0, 16384)
        with nc.Block() as blk:
            @blk.vector
            def _(v):
                for ti in range(NT):
                    arena, colb = hs_tile(ti)
                    v.tensor_reduce(ap(SM, S_TMP + ti, 1), ap(arena, colb, 4096), AX.X,
                                    ALU.max, apply_absolute_value=True)
                v.drain()
                v.tensor_scalar(ap(SM, S_TMP, 16), ap(SM, S_TMP, 16), 1e-5, 0.0, ALU.max, ALU.add)
                v.drain()
                v.tensor_scalar(ap(SM, S_SCL, 16), ap(SM, S_TMP, 16), 1.0 / 127.0, 0.0,
                                ALU.mult, ALU.add)
                v.drain()
                v.reciprocal(ap(SM, S_ST2, 16), ap(SM, S_SCL, 16))
                v.drain()
                for ti in range(NT):
                    arena, colb = hs_tile(ti)
                    v.tensor_scalar(ap(Ff32, 0, 2048), ap(arena, colb, 2048),
                                    ap(SM, S_ST2 + ti, 1), MAGIC, ALU.mult, ALU.add)
                    v.tensor_scalar(ap(Ff32, 2048, 2048), ap(arena, colb + 2048, 2048),
                                    ap(SM, S_ST2 + ti, 1), MAGIC, ALU.mult, ALU.add)
                    v.drain()
                    v.tensor_scalar(ap(Ff32, 0, 2048), ap(Ff32, 0, 2048), MAGIC, 0.0,
                                    ALU.subtract, ALU.add)
                    v.tensor_scalar(ap(Ff32, 2048, 2048), ap(Ff32, 2048, 2048), MAGIC, 0.0,
                                    ALU.subtract, ALU.add)
                    v.drain()
                    v.tensor_scalar(ap(arena, colb, 2048), ap(Ff32, 0, 2048),
                                    ap(SM, S_SCL + ti, 1), 1.0, ALU.mult, ALU.mult)
                    v.tensor_scalar(ap(arena, colb + 2048, 2048), ap(Ff32, 2048, 2048),
                                    ap(SM, S_SCL + ti, 1), 1.0, ALU.mult, ALU.mult)
                    v.drain()
        if "xq2C" in tapd:
            emit_tap(tapd["xq2C"], C, 0, 16384)
        groups = [(0, 1, 2), (3, 4, 5), (6, 7, 8), (9, 10, 11), (12, 13, 14), (15,)]
        dst_dram = out_d if out_dst == "out" else x_state
        for grp in groups:
            ng = len(grp)
            with nc.Block() as blk:
                base_mm, base_ev, base_ac, base_sy = P.val(MM), P.val(EV), P.val(AC), P.val(SY)
                W2T = "w2t"
                base_w2t = P.val(W2T)
                @blk.tensor
                def _(t):
                    for gi, ti in enumerate(grp):
                        arena, colb = hs_tile(ti)
                        for b in range(4):
                            idx = gi * 4 + b
                            pt = (pt0, pt1)[idx % 2]
                            if idx >= 2:
                                t.wait_ge(P.S(EV), base_ev + idx - 1)
                            for s8 in range(8):
                                ins = t.transpose(pt[:, s8 * 128:(s8 + 1) * 128],
                                                  ap(arena, colb + (b * 8 + s8) * 128, 128),
                                                  ap(CSB, IDB, 128))
                            ins.then_inc(P.S(MM), 1)
                            P.cnt[MM] += 1
                    t.wait_ge(P.S(EV), base_ev + ng * 4)
                    for h2 in range(32):
                        t.wait_ge(P.S(W2T), base_w2t + (h2 + 1) * 16)
                        for gi in range(ng):
                            for dt_ in range(2):
                                ins = t.matmul(ps[gi * 2 + dt_][:, :],
                                               ap(A, h2 * (ng * 128) + gi * 128, 128),
                                               ap(Fb, WT + (h2 % 2) * 1024 + dt_ * 512, 512),
                                               start=(h2 == 0), stop=(h2 == 31))
                        ins.then_inc(P.S(MM), 1)
                        P.cnt[MM] += 1
                @blk.vector
                def _(v):
                    for gi, ti in enumerate(grp):
                        for b in range(4):
                            idx = gi * 4 + b
                            pt = (pt0, pt1)[idx % 2]
                            v.wait_ge(P.S(MM), base_mm + idx + 1)
                            dst = bass.AP(A, (b * 8) * (ng * 128) + gi * 128,
                                          [[A.shape[1], 128], [ng * 128, 8], [1, 128]])
                            srcp = bass.AP(pt, 0, [[1024, 128], [128, 8], [1, 128]])
                            ins = v.tensor_copy(dst, srcp)
                            P.inc(ins, EV)
                    for gi, ti in enumerate(grp):
                        v.wait_ge(P.S(MM), base_mm + ng * 4 + 32)
                        yar, ycol = hs_f32(ti)
                        for dt_ in range(2):
                            ins = v.tensor_scalar(ap(yar, ycol + dt_ * 512, 512),
                                                  ps[gi * 2 + dt_][:, :], ap(SM, S_WS + 3, 1),
                                                  1.0, ALU.mult, ALU.mult)
                        v.drain()
                        ins = v.reduce_sum(ap(SM, S_SUM + ti, 1), ap(yar, ycol, 1024), AX.X)
                        v.drain()
                        P.inc(ins, EV)
                @blk.sync
                def _(sy):
                    for h2 in range(32):
                        if h2 >= 2:
                            sy.wait_ge(P.S(MM), base_mm + ng * 4 + h2 - 1)
                        P.inc(sy.dma_start(ap(Fb, WT + (h2 % 2) * 1024, 1024),
                                           bass.AP(w2t_spill, h2 * 128 * 1024, [[1024, 128], [1, 1024]])),
                              W2T, 16)
            for ti in grp:
                yar, ycol = hs_f32(ti)
                with nc.Block() as blk2:
                    base_ac2, base_ev2, base_sy2 = P.val(AC), P.val(EV), P.val(SY)
                    @blk2.sync
                    def _(sy):
                        P.inc(sy.dma_start(ap(Ff32, 1024, 1024),
                                           x_state[ti * 128:(ti + 1) * 128, :]), SY, 16)
                    @blk2.scalar
                    def _(s):
                        ins = s.activation(ap(H, 0, 1024), ap(yar, ycol, 1024), AF.Square,
                                           accum_out=ap(SM, S_SQ + ti, 1))
                        P.inc(ins, AC)
                    @blk2.vector
                    def _(v):
                        v.wait_ge(P.S(AC), base_ac2 + 1)
                        v.tensor_scalar(ap(SM, S_MEAN + ti, 1), ap(SM, S_SUM + ti, 1), 1.0 / D,
                                        0.0, ALU.mult, ALU.add)
                        v.drain()
                        v.tensor_tensor(ap(SM, S_TMP + ti, 1), ap(SM, S_MEAN + ti, 1),
                                        ap(SM, S_MEAN + ti, 1), ALU.mult)
                        v.drain()
                        v.scalar_tensor_tensor(ap(SM, S_VAR + ti, 1), ap(SM, S_SQ + ti, 1), 1.0 / D,
                                               ap(SM, S_TMP + ti, 1), ALU.mult, ALU.subtract)
                        v.drain()
                        v.tensor_scalar(ap(SM, S_VAR + ti, 1), ap(SM, S_VAR + ti, 1), EPS, 0.0,
                                        ALU.add, ALU.add)
                        v.drain()
                        ins = v.reciprocal(ap(SM, S_TMP + ti, 1), ap(SM, S_VAR + ti, 1))
                        P.inc(ins, EV)
                    @blk2.scalar
                    def _(s2):
                        s2.wait_ge(P.S(EV), base_ev2 + 1)
                        ins = s2.activation(ap(SM, S_R2 + ti, 1), ap(SM, S_TMP + ti, 1), AF.Sqrt)
                        P.inc(ins, AC)
                    @blk2.vector
                    def _(v2):
                        v2.wait_ge(P.S(AC), base_ac2 + 2)
                        ins = v2.scalar_tensor_tensor(ap(SM, S_NMR2 + ti, 1), ap(SM, S_MEAN + ti, 1),
                                                      -1.0, ap(SM, S_R2 + ti, 1), ALU.mult, ALU.mult)
                        v2.drain()
                        P.inc(ins, EV)
                    @blk2.scalar
                    def _(s3):
                        s3.wait_ge(P.S(EV), base_ev2 + 2)
                        ins = s3.activation(ap(Ff32, 0, 1024), ap(yar, ycol, 1024), AF.Identity,
                                            bias=ap(SM, S_NMR2 + ti, 1), scale=ap(SM, S_R2 + ti, 1))
                        P.inc(ins, AC)
                    @blk2.vector
                    def _(v3):
                        v3.wait_ge(P.S(AC), base_ac2 + 3)
                        v3.wait_ge(P.S(SY), base_sy2 + 16)
                        ins = v3.tensor_tensor(ap(Ff32, 0, 1024), ap(Ff32, 0, 1024),
                                               ap(Ff32, 1024, 1024), ALU.add)
                        P.inc(ins, EV)
                    @blk2.gpsimd
                    def _(g):
                        g.wait_ge(P.S(EV), P.val(EV))
                        P.inc(g.dma_start(dst_dram[ti * 128:(ti + 1) * 128, :], ap(Ff32, 0, 1024)),
                              ST, 16)
                        g.wait_ge(P.S(ST), P.val(ST))
        if "yC" in tapd:
            emit_tap(tapd["yC"], CF32, 0, 8192)

    emit_ln(x_in)
    if "xn_tm" in tapd:
        emit_tap(tapd["xn_tm"], C, 0, NT * D)
    emit_transpose_tm_fm(C, 0, 1024, A, 0, TOK, NT, NCH)
    if "xn_fm" in tapd:
        emit_tap(tapd["xn_fm"], A, 0, NCH * TOK)
    emit_l0_attn()
    if n_blocks >= 2:
        emit_l0_ffn(out_dst=None)
    if n_blocks >= 3:
        emit_l1_attn(out_dst=None)
    if n_blocks >= 4:
        emit_l1_ffn(out_dst="out")

    if n_blocks <= 3:
        with nc.Block() as blk:
            @blk.gpsimd
            def _(g):
                for ti in range(NT):
                    P.inc(g.dma_start(ap(G, (ti % 4) * 1024, 1024), x_state[ti * 128:(ti + 1) * 128, :]), GQ, 16)
                    g.wait_ge(P.S(GQ), P.val(GQ))
                    P.inc(g.dma_start(out_d[ti * 128:(ti + 1) * 128, :], ap(G, (ti % 4) * 1024, 1024)), ST, 16)
                g.wait_ge(P.S(ST), P.val(ST))

    for gu in P._guards:
        gu.__exit__(None, None, None)
    return nc


def host_consts():
    ident = np.eye(128, dtype=np.float32)
    import ml_dtypes
    identb = ident.astype(ml_dtypes.bfloat16)
    identf = np.eye(2, dtype=np.float32)
    p = np.arange(128)[:, None]
    f = np.arange(512)[None, :]
    bands = []
    for i in range(28):
        d0 = (i - 12) * 128
        bands.append(np.abs(d0 + p - f))
    dm = np.concatenate(bands, axis=1).astype(np.float16)
    return identb, identf, dm


def make_in_maps(inputs, n_cores=8):
    identb, identf, dm = host_consts()
    base = dict(
        w_down=np.ascontiguousarray(inputs["L0_attn_W_down"]),
        tokeys=np.ascontiguousarray(inputs["L0_attn_tokeys"]),
        toquer=np.ascontiguousarray(inputs["L0_attn_toqueries"]),
        tovals=np.ascontiguousarray(inputs["L0_attn_tovalues"]),
        w_uni=np.ascontiguousarray(inputs["L0_attn_W_unify"]),
        b_uni=np.ascontiguousarray(inputs["L0_attn_b_unify"]),
        wg1=np.ascontiguousarray(inputs["L0_ffn_wg1"]),
        bg1=np.ascontiguousarray(inputs["L0_ffn_bg1"]),
        wg2=np.ascontiguousarray(inputs["L0_ffn_wg2"]),
        w_ine=np.ascontiguousarray(inputs["L0_ffn_w_in"]),
        w_oute=np.ascontiguousarray(inputs["L0_ffn_w_out"]),
        wqkv=np.ascontiguousarray(inputs["L1_attn_Wqkv"]),
        bqkv=np.ascontiguousarray(inputs["L1_attn_bqkv"]),
        wout=np.ascontiguousarray(inputs["L1_attn_Wout"]),
        bout=np.ascontiguousarray(inputs["L1_attn_bout"]),
        w1=np.ascontiguousarray(inputs["L1_ffn_w1"]),
        convw=np.ascontiguousarray(inputs["L1_ffn_conv_w"]),
        convb=np.ascontiguousarray(inputs["L1_ffn_conv_b"]),
        w2=np.ascontiguousarray(inputs["L1_ffn_w2"]),
        ident=identb, identf=identf, dmast=dm,
    )
    x = np.asarray(inputs["L0_attn_x"])
    return [dict(base, x=np.ascontiguousarray(x[i])) for i in range(n_cores)]


def kernel(**inputs):
    nc = build(n_blocks=N_BLOCKS, taps=DEBUG_TAPS)
    in_maps = make_in_maps(inputs)
    res = run_bass_kernel_spmd(nc, in_maps, core_ids=list(range(8)))
    out = np.stack([np.asarray(r["out"]) for r in res.results]).astype(np.float32)
    return out


# version 05
# speedup vs baseline: 1.0266x; 1.0266x over previous
import math
import numpy as np
import concourse.bass as bass
import concourse.mybir as mybir
from concourse.bass_utils import run_bass_kernel_spmd

F32 = mybir.dt.float32
BF16 = mybir.dt.bfloat16
F16 = mybir.dt.float16
AF = mybir.ActivationFunctionType
ALU = mybir.AluOpType
AX = mybir.AxisListType

TOK, D = 2048, 1024
NT, NCH, NQT = 16, 8, 4
HEADS, NPAIR = 16, 8
EPS = 1e-5
MAGIC = 12582912.0

N_BLOCKS = 4
DEBUG_TAPS = ()


def alibi_slopes16():
    def pow2(n):
        start = 2 ** (-(2 ** (-(math.log2(n) - 3))))
        return [start ** i for i in range(n)]
    return np.asarray(pow2(16), dtype=np.float32)


SLOPES = [float(s) for s in alibi_slopes16()]
SKIP_THRESH = 45.0


def band_minabs(kt, qt):
    d0 = 128 * kt - 512 * qt
    lo, hi = d0 - 511, d0 + 127
    if lo <= 0 <= hi:
        return 0
    return min(abs(lo), abs(hi))


def pair_sched(pair, l1):
    out = []
    for qt in range(NQT):
        kl = []
        for ktp in range(8):
            kts = [2 * ktp, 2 * ktp + 1]
            if l1:
                hA, hB = 2 * pair, 2 * pair + 1
                keep = any(SLOPES[h] * band_minabs(kt, qt) < SKIP_THRESH
                           for h in (hA, hB) for kt in kts)
            else:
                keep = True
            if keep:
                kl.append(ktp)
        out.append(kl)
    return out


class Ctx:
    def __init__(self, nc):
        self.nc = nc
        self.sem = {}
        self.cnt = {}
        self._guards = []

    def S(self, name):
        if name not in self.sem:
            g = self.nc.semaphore(name)
            self.sem[name] = g.__enter__()
            self._guards.append(g)
            self.cnt[name] = 0
        return self.sem[name]

    def inc(self, ins, name, k=1):
        ins.then_inc(self.S(name), k)
        self.cnt[name] += k
        return self.cnt[name]

    def val(self, name):
        self.S(name)
        return self.cnt[name]


def build(n_blocks=N_BLOCKS, taps=()):
    nc = bass.Bass("TRN2", target_bir_lowering=False)
    P = Ctx(nc)

    x_in = nc.dram_tensor("x", [TOK, D], F32, kind="ExternalInput")
    w_down = nc.dram_tensor("w_down", [D, D], F32, kind="ExternalInput")
    tokeys = nc.dram_tensor("tokeys", [HEADS, 64, 64], F32, kind="ExternalInput")
    toquer = nc.dram_tensor("toquer", [HEADS, 64, 64], F32, kind="ExternalInput")
    tovals = nc.dram_tensor("tovals", [HEADS, 64, 64], F32, kind="ExternalInput")
    w_uni = nc.dram_tensor("w_uni", [D, D], F32, kind="ExternalInput")
    b_uni = nc.dram_tensor("b_uni", [D], F32, kind="ExternalInput")
    wg1 = nc.dram_tensor("wg1", [D, 256], F32, kind="ExternalInput")
    bg1 = nc.dram_tensor("bg1", [256], F32, kind="ExternalInput")
    wg2 = nc.dram_tensor("wg2", [256, 8], F32, kind="ExternalInput")
    w_ine = nc.dram_tensor("w_ine", [8, D, 2048], F32, kind="ExternalInput")
    w_oute = nc.dram_tensor("w_oute", [8, 2048, D], F32, kind="ExternalInput")
    wqkv = nc.dram_tensor("wqkv", [D, 3 * D], F32, kind="ExternalInput")
    bqkv = nc.dram_tensor("bqkv", [3 * D], F32, kind="ExternalInput")
    wout_d = nc.dram_tensor("wout", [D, D], F32, kind="ExternalInput")
    bout = nc.dram_tensor("bout", [D], F32, kind="ExternalInput")
    w1d = nc.dram_tensor("w1", [4096, D], F32, kind="ExternalInput")
    convw = nc.dram_tensor("convw", [4096, 1, 3], F32, kind="ExternalInput")
    convb = nc.dram_tensor("convb", [4096], F32, kind="ExternalInput")
    w2d = nc.dram_tensor("w2", [D, 4096], F32, kind="ExternalInput")
    ident_in = nc.dram_tensor("ident", [128, 128], BF16, kind="ExternalInput")
    identf_in = nc.dram_tensor("identf", [2, 2], F32, kind="ExternalInput")
    dmast = nc.dram_tensor("dmast", [128, 28 * 512], F16, kind="ExternalInput")

    out_d = nc.dram_tensor("out", [TOK, D], F32, kind="ExternalOutput")
    x_state = nc.dram_tensor("x_state", [TOK, D], F32)
    w2t_spill = nc.dram_tensor("w2t_spill", [128, 32 * 1024], BF16, kind="ExternalOutput" if "w2t" in taps else "Internal")

    tapd = {}
    for t in taps:
        if t == 'w2t':
            continue
        shp, dt_ = {
            "xn_tm": ([128, NT * D], BF16), "xn_fm": ([128, NCH * TOK], BF16),
            "xd_fm": ([128, NCH * TOK], BF16), "av_fm": ([128, NCH * TOK], BF16),
            "kp": ([128, TOK], BF16), "qp": ([128, TOK], BF16), "vp": ([128, TOK], BF16),
            "gates": ([128, NT * 8], F32), "u_dbg": ([128, NT * D], F32),
            "h_dbg": ([128, 16 * 512], BF16), "hs_tm": ([128, NT * 4096], BF16),
            "sm1": ([128, 384], F32), "t_dbg": ([128, 4096], F32),
            "smw": ([128, 384], F32), "hsC": ([128, 16384], BF16), "xqA": ([128, 16384], BF16),
            "xq2C": ([128, 16384], BF16), "yC": ([128, 8192], F32),
            "hbq": ([128, 6144], BF16), "ff": ([128, 4096], F32),
        }[t]
        tapd[t] = nc.dram_tensor("tap_" + t, shp, dt_, kind="ExternalOutput")

    CSB = nc.alloc_sbuf_tensor_at("CSBbf", [128, 2816], BF16, offset=16896)
    CST = nc.alloc_sbuf_tensor_at("CSTf32", [128, 512], F32, offset=22528)
    SM = nc.alloc_sbuf_tensor_at("SMf32", [128, 384], F32, offset=24576)
    ONF = nc.alloc_sbuf_tensor_at("ONF", [128, 128], F32, offset=26112)
    IDF = nc.alloc_sbuf_tensor_at("IDF2", [2, 16], F32, offset=26624)
    VT = nc.alloc_sbuf_tensor_at("VTbf", [128, 2816], BF16, offset=26688)
    Hb = nc.alloc_sbuf_tensor_at("Hbf", [128, 6144], BF16, offset=32320)
    Hbf32 = nc.alloc_sbuf_tensor_at("Hbf32", [128, 512], F32, offset=36480)
    H = nc.alloc_sbuf_tensor_at("Hf32", [128, 1024], F32, offset=44608)
    SMI = nc.alloc_sbuf_tensor_at("SMI", [128, 128], mybir.dt.int32, offset=46656)
    A = nc.alloc_sbuf_tensor_at("Abf", [128, 16384], BF16, offset=48704)
    G = nc.alloc_sbuf_tensor_at("Gf32", [128, 16384], F32, offset=48704)
    DM = nc.alloc_sbuf_tensor_at("DMf16", [128, 14336], F16, offset=48704)
    Bb = nc.alloc_sbuf_tensor_at("Bbf", [128, 16384], BF16, offset=81472)
    YS0 = nc.alloc_sbuf_tensor_at("ysb0", [128, 8192], F32, offset=81472)
    C = nc.alloc_sbuf_tensor_at("Cbf", [128, 16384], BF16, offset=114240)
    CF32 = nc.alloc_sbuf_tensor_at("Cf32", [128, 8192], F32, offset=114240)
    Dd = nc.alloc_sbuf_tensor_at("Dbf", [128, 16384], BF16, offset=147008)
    DF32 = nc.alloc_sbuf_tensor_at("Df32", [128, 8192], F32, offset=147008)
    E = nc.alloc_sbuf_tensor_at("Ebf", [128, 16384], BF16, offset=179776)
    Ef32 = nc.alloc_sbuf_tensor_at("Ef32", [128, 8192], F32, offset=179776)
    YS1 = nc.alloc_sbuf_tensor_at("ysb1", [128, 8192], F32, offset=179776)
    Fb = nc.alloc_sbuf_tensor_at("Fbf", [128, 8192], BF16, offset=212544)
    Ff32 = nc.alloc_sbuf_tensor_at("Ff32", [128, 4096], F32, offset=212544)

    IDB, TOKT = 0, 128
    E2B = TOKT + 1536
    WG2B = E2B + 2
    BUNIB = WG2B + 16
    ONES2, E2T = 0, 128
    BQKV = E2T + 128
    BG1C = BQKV + 24
    CONVW = BG1C + 2
    CONVB = CONVW + 96
    S_SUM, S_SQ, S_MEAN, S_VAR, S_R, S_NMR, S_TMP = 0, 16, 32, 48, 64, 80, 96
    S_QM, S_C2, S_NC = 112, 120, 124
    S_SCL = 128
    S_ST2 = 144
    S_WS = 160
    S_G = 176
    S_R2, S_NMR2 = 304, 320
    S_LGT = 336

    def ap(t, c0, w, p0=0, pn=128):
        return t[p0:p0 + pn, c0:c0 + w]

    pw0 = nc.alloc_psum_tensor("pw0", [128, 1024], F32)
    pw1 = nc.alloc_psum_tensor("pw1", [128, 1024], F32)
    pn0 = nc.alloc_psum_tensor("pn0", [128, 512], F32)
    pn1 = nc.alloc_psum_tensor("pn1", [128, 512], F32)
    ps = [pw0[:, 0:512], pw0[:, 512:1024], pw1[:, 0:512], pw1[:, 512:1024],
          pn0[:, :], pn1[:, :]]
    pt0 = nc.alloc_psum_tensor("pt0", [128, 1024], BF16)
    pt1 = nc.alloc_psum_tensor("pt1", [128, 1024], BF16)

    SY, GQ, MM, EV, AC, ST = "syq", "gq", "mm", "ev", "ac", "st"

    ncd = nc.allow_non_contiguous_dma(reason="small constant loads")
    ncd.__enter__()
    with nc.Block() as blk:
        @blk.gpsimd
        def _(g):
            P.inc(g.dma_start(ap(CSB, IDB, 128), ident_in[:, :]), GQ, 16)
            P.inc(g.dma_start(IDF[0:2, 0:2], identf_in[:, :]), GQ, 16)
            for pi, T in enumerate((tokeys, toquer, tovals)):
                for h in range(HEADS):
                    pair, sub = h // 2, h % 2
                    col = TOKT + (pi * 8 + pair) * 64
                    src = bass.AP(T, h * 4096, [[1, 64], [64, 64]])
                    P.inc(g.dma_start(ap(CSB, col, 64, p0=sub * 64, pn=64), src,
                                      ), GQ, 16)
            g.memset(ap(CSB, E2B, 1, p0=0, pn=64), 1.0)
            g.memset(ap(CSB, E2B, 1, p0=64, pn=64), 0.0)
            g.memset(ap(CSB, E2B + 1, 1, p0=0, pn=64), 0.0)
            g.memset(ap(CSB, E2B + 1, 1, p0=64, pn=64), 1.0)
            for c in range(2):
                src = bass.AP(wg2, c * 128 * 8, [[8, 128], [1, 8]])
                P.inc(g.dma_start(ap(CSB, WG2B + c * 8, 8), src), GQ, 16)
            g.memset(ONF[:, :], 1.0)
            g.memset(ap(CST, ONES2, 128, pn=2), 1.0)
            P.inc(g.dma_start(ap(CST, BQKV, 24), bqkv.ap().rearrange("(c p) -> p c", p=128)), GQ, 16)
            P.inc(g.dma_start(ap(CST, BG1C, 2), bg1.ap().rearrange("(c p) -> p c", p=128)), GQ, 16)
            src = bass.AP(convw, 0, [[3, 128], [384, 32], [1, 3]])
            P.inc(g.dma_start(ap(CST, CONVW, 96), src), GQ, 16)
            P.inc(g.dma_start(ap(CST, CONVB, 32), convb.ap().rearrange("(c p) -> p c", p=128)), GQ, 16)
            g.memset(VT[:, :], 0.0)
            for k in range(NCH):
                P.inc(g.dma_start(ap(Fb, k * 1024, 1024), w_down[k * 128:(k + 1) * 128, :]), GQ, 16)
            g.wait_ge(P.S(GQ), P.val(GQ))
        @blk.sync
        def _(sy):
            P.inc(sy.dma_start(ap(H, 0, 1024, pn=1), b_uni.ap()[None, :]), SY, 16)
    ncd.__exit__(None, None, None)
    with nc.Block() as blk:
        @blk.gpsimd
        def _(g):
            for kt in range(16):
                g.memset(ap(VT, kt * 80 + 64, 1), 1.0)

    with nc.Block() as blk:
        @blk.tensor
        def _(t):
            t.wait_ge(P.S(SY), P.val(SY))
            for dt_ in range(2):
                ins = t.matmul(ps[dt_][:, :], ONF[0:1, :], ap(H, dt_ * 512, 512, pn=1),
                               start=True, stop=True, tile_position=(0, 0))
            P.inc(ins, MM)
        @blk.vector
        def _(v):
            v.wait_ge(P.S(MM), P.val(MM))
            for dt_ in range(2):
                v.tensor_copy(ap(CSB, BUNIB + dt_ * 512, 512), ps[dt_][:, :])

    def emit_ln(src_dram, quant=False):
        with nc.Block() as blk:
            base_sy = P.val(SY)
            @blk.sync
            def _(sy):
                for t in range(NT):
                    P.inc(sy.dma_start(ap(G, t * 1024, 1024), src_dram[t * 128:(t + 1) * 128, :]), SY, 16)
            @blk.vector
            def _(v):
                for t in range(NT):
                    v.wait_ge(P.S(SY), base_sy + 16 * (t + 1))
                    ins = v.reduce_sum(ap(SM, S_SUM + t, 1), ap(G, t * 1024, 1024), AX.X)
                P.inc(ins, EV)
            @blk.scalar
            def _(s):
                for t in range(NT):
                    s.wait_ge(P.S(SY), base_sy + 16 * (t + 1))
                    ins = s.activation(ap(H, 0, 1024), ap(G, t * 1024, 1024), AF.Square,
                                       accum_out=ap(SM, S_SQ + t, 1))
                P.inc(ins, AC)
        with nc.Block() as blk:
            @blk.vector
            def _(v):
                v.tensor_scalar(ap(SM, S_MEAN, 16), ap(SM, S_SUM, 16), 1.0 / D, 0.0, ALU.mult, ALU.add)
                v.drain()
                v.tensor_tensor(ap(SM, S_TMP, 16), ap(SM, S_MEAN, 16), ap(SM, S_MEAN, 16), ALU.mult)
                v.drain()
                v.scalar_tensor_tensor(ap(SM, S_VAR, 16), ap(SM, S_SQ, 16), 1.0 / D, ap(SM, S_TMP, 16),
                                       ALU.mult, ALU.subtract)
                v.drain()
                v.tensor_scalar(ap(SM, S_VAR, 16), ap(SM, S_VAR, 16), EPS, 0.0, ALU.add, ALU.add)
                v.drain()
                ins = v.reciprocal(ap(SM, S_TMP, 16), ap(SM, S_VAR, 16))
                P.inc(ins, EV)
            @blk.scalar
            def _(s):
                s.wait_ge(P.S(EV), P.val(EV))
                ins = s.activation(ap(SM, S_R, 16), ap(SM, S_TMP, 16), AF.Sqrt)
                P.inc(ins, AC)
        with nc.Block() as blk:
            @blk.vector
            def _(v):
                ins = v.scalar_tensor_tensor(ap(SM, S_NMR, 16), ap(SM, S_MEAN, 16), -1.0,
                                             ap(SM, S_R, 16), ALU.mult, ALU.mult)
                P.inc(ins, EV)
        if not quant:
            with nc.Block() as blk:
                @blk.scalar
                def _(s):
                    for t in range(NT):
                        ins = s.activation(ap(C, t * 1024, 1024), ap(G, t * 1024, 1024), AF.Identity,
                                           bias=ap(SM, S_NMR + t, 1), scale=ap(SM, S_R + t, 1))
                    P.inc(ins, AC)
        else:
            with nc.Block() as blk:
                base_ac = P.val(AC)
                @blk.scalar
                def _(s):
                    for t in range(NT):
                        ins = s.activation(ap(G, t * 1024, 1024), ap(G, t * 1024, 1024), AF.Identity,
                                           bias=ap(SM, S_NMR + t, 1), scale=ap(SM, S_R + t, 1))
                        P.inc(ins, AC)
                @blk.vector
                def _(v):
                    for t in range(NT):
                        v.wait_ge(P.S(AC), base_ac + t + 1)
                        v.tensor_reduce(ap(SM, S_TMP + t, 1), ap(G, t * 1024, 1024), AX.X,
                                        ALU.max, apply_absolute_value=True)
                    v.drain()
                    v.tensor_scalar(ap(SM, S_TMP, 16), ap(SM, S_TMP, 16), 1e-5, 0.0, ALU.max, ALU.add)
                    v.drain()
                    v.tensor_scalar(ap(SM, S_SCL, 16), ap(SM, S_TMP, 16), 1.0 / 127.0, 0.0, ALU.mult, ALU.add)
                    v.drain()
                    v.reciprocal(ap(SM, S_ST2, 16), ap(SM, S_SCL, 16))
                    v.drain()
                    for t in range(NT):
                        v.tensor_scalar(ap(H, 0, 1024), ap(G, t * 1024, 1024), ap(SM, S_ST2 + t, 1),
                                        MAGIC, ALU.mult, ALU.add)
                        v.drain()
                        v.tensor_scalar(ap(H, 0, 1024), ap(H, 0, 1024), MAGIC, 0.0,
                                        ALU.subtract, ALU.add)
                        v.drain()
                        v.tensor_scalar(ap(C, t * 1024, 1024), ap(H, 0, 1024), ap(SM, S_SCL + t, 1),
                                        1.0, ALU.mult, ALU.mult)
                        v.drain()
                    P.inc(v.engine_nop(), EV)

    def emit_transpose_tm_fm(src, src_base, row_w, dst, dst_base, tok_w, nt, nch_):
        with nc.Block() as blk:
            base_ev = P.val(EV)
            @blk.tensor
            def _(t):
                for ti in range(nt):
                    pt = (pt0, pt1)[ti % 2]
                    if ti >= 2:
                        t.wait_ge(P.S(EV), base_ev + ti - 1)
                    for c in range(nch_):
                        ins = t.transpose(pt[:, c * 128:(c + 1) * 128],
                                          ap(src, src_base + ti * row_w + c * 128, 128),
                                          ap(CSB, IDB, 128))
                    P.inc(ins, MM)
            base_mm = P.val(MM) - nt
            @blk.vector
            def _(v):
                for ti in range(nt):
                    pt = (pt0, pt1)[ti % 2]
                    v.wait_ge(P.S(MM), base_mm + ti + 1)
                    dst_ap = bass.AP(dst, dst_base + ti * 128,
                                     [[dst.shape[1], 128], [tok_w, nch_], [1, 128]])
                    ins = v.tensor_copy(dst_ap, pt[:, 0:nch_ * 128])
                    P.inc(ins, EV)

    def emit_proj_fm(win_sb, win_base, xin, xin_base, dst, dst_base, nj,
                     act=None, bias_cols=None, bias_base=0, tok_w=2048, kstride=1024):
        with nc.Block() as blk:
            base_ev = P.val(EV)
            n_grp = nj * 4
            @blk.tensor
            def _(t):
                grp = 0
                for j in range(nj):
                    for ti in range(4):
                        slot = grp % 6
                        if grp >= 6:
                            t.wait_ge(P.S(EV), base_ev + grp - 5)
                        for k in range(NCH):
                            ins = t.matmul(ps[slot][:, :],
                                           ap(win_sb, win_base + k * kstride + j * 128, 128),
                                           ap(xin, xin_base + k * tok_w + ti * 512, 512),
                                           start=(k == 0), stop=(k == NCH - 1))
                        P.inc(ins, MM)
                        grp += 1
            base_mm = P.val(MM) - n_grp
            eng = "scalar" if act is not None else "vector"
            if act is None:
                @blk.vector
                def _(v):
                    grp = 0
                    for j in range(nj):
                        for ti in range(4):
                            slot = grp % 6
                            v.wait_ge(P.S(MM), base_mm + grp + 1)
                            ins = v.tensor_copy(ap(dst, dst_base + j * tok_w + ti * 512, 512),
                                                ps[slot][:, :])
                            P.inc(ins, EV)
                            grp += 1
            else:
                @blk.scalar
                def _(s):
                    grp = 0
                    for j in range(nj):
                        for ti in range(4):
                            slot = grp % 6
                            s.wait_ge(P.S(MM), base_mm + grp + 1)
                            kw = {}
                            if bias_cols is not None:
                                kw["bias"] = ap(bias_cols, bias_base + j, 1)
                            ins = s.activation(ap(dst, dst_base + j * tok_w + ti * 512, 512),
                                               ps[slot][:, :], act, **kw)
                            P.inc(ins, EV)
                            grp += 1

    def emit_tap(dram, sb, base, width):
        with nc.Block() as blk:
            @blk.gpsimd
            def _(g):
                P.inc(g.dma_start(dram[:, 0:width], ap(sb, base, width)), ST, 16)
                g.wait_ge(P.S(ST), P.val(ST))

    def emit_l0_attn():
        emit_proj_fm(Fb, 0, A, 0, Bb, 0, NCH)
        if "xd_fm" in tapd:
            emit_tap(tapd["xd_fm"], Bb, 0, NCH * TOK)
        with nc.Block() as blk:
            @blk.gpsimd
            def _(g):
                for k in range(NCH):
                    P.inc(g.dma_start(ap(Fb, k * 1024, 1024), w_uni[k * 128:(k + 1) * 128, :]), GQ, 16)
        KP, QP, VP, SQc = 0, 2048, 4096, 2048
        for pair in range(NPAIR):
            with nc.Block() as blk:
                base_ev = P.val(EV)
                @blk.tensor
                def _(t):
                    g2 = 0
                    for ti in range(4):
                        for pi in range(3):
                            slot = g2 % 6
                            if g2 >= 6:
                                t.wait_ge(P.S(EV), base_ev + g2 - 5)
                            col = TOKT + (pi * 8 + pair) * 64
                            ins = t.matmul(ps[slot][0:64, :], ap(CSB, col, 64, p0=0, pn=64),
                                           ap(Bb, pair * 2048 + ti * 512, 512, p0=0, pn=64),
                                           start=True, stop=True, tile_position=(0, 0))
                            ins = t.matmul(ps[slot][64:128, :], ap(CSB, col, 64, p0=64, pn=64),
                                           ap(Bb, pair * 2048 + ti * 512, 512, p0=64, pn=64),
                                           start=True, stop=True)
                            P.inc(ins, MM)
                            g2 += 1
                base_mm = P.val(MM) - 12
                @blk.vector
                def _(v):
                    g2 = 0
                    for ti in range(4):
                        for pi in range(3):
                            slot = g2 % 6
                            v.wait_ge(P.S(MM), base_mm + g2 + 1)
                            dstc = (KP, QP, VP)[pi] + ti * 512
                            ins = v.tensor_copy(ap(Hb, dstc, 512), ps[slot][:, :])
                            P.inc(ins, EV)
                            g2 += 1
            if pair == 0 and "kp" in tapd:
                emit_tap(tapd["kp"], Hb, KP, TOK)
                emit_tap(tapd["qp"], Hb, QP, TOK)
                emit_tap(tapd["vp"], Hb, VP, TOK)
            emit_v_transpose(Hb, VP)
            emit_negC(KP, QP, SQc, 0.0625, Hb)
            emit_attn_pair2(pair, l1=False, k_cols=KP, q_cols=QP)
        if "av_fm" in tapd:
            emit_tap(tapd["av_fm"], C, 0, NCH * TOK)
        emit_unify_blockend(Fb, 0, CSB, BUNIB, x_in, first=True)

    def emit_vt_init():
        with nc.Block() as blk:
            @blk.gpsimd
            def _(g):
                g.memset(VT[:, :], 0.0)
        with nc.Block() as blk:
            @blk.gpsimd
            def _(g):
                for kt in range(16):
                    g.memset(ap(VT, kt * 80 + 64, 1), 1.0)

    def emit_v_transpose(v_t, VP):
        with nc.Block() as blk:
            base_mm = P.val(MM)
            @blk.tensor
            def _(t):
                for half in range(2):
                    pt = (pt0, pt1)[half]
                    for k8 in range(8):
                        kt = half * 8 + k8
                        ins = t.transpose(pt[:, k8 * 128:(k8 + 1) * 128],
                                          ap(v_t, VP + kt * 128, 128), ap(CSB, IDB, 128))
                    P.inc(ins, MM)
            @blk.vector
            def _(v):
                for half in range(2):
                    pt = (pt0, pt1)[half]
                    v.wait_ge(P.S(MM), base_mm + half + 1)
                    srcA = bass.AP(pt, 0, [[1024, 128], [128, 8], [1, 64]])
                    dstA = bass.AP(VT, half * 8 * 80, [[VT.shape[1], 128], [80, 8], [1, 64]])
                    v.tensor_copy(dstA, srcA)
                    srcB = bass.AP(pt, 64, [[1024, 128], [128, 8], [1, 64]])
                    dstB = bass.AP(VT, 1280 + half * 8 * 64, [[VT.shape[1], 128], [64, 8], [1, 64]])
                    ins = v.tensor_copy(dstB, srcB)
                    P.inc(ins, EV)

    def emit_negC(KP, QP, SQc, coef, kq_t, sq_t=None):
        sq_t = sq_t if sq_t is not None else E
        """negC for both heads -> SM[S_NC..+2]: C = coef*(max|q|^2 + max|k|^2) per head."""
        kq_list = kq_t if isinstance(kq_t, (list, tuple)) else (kq_t, kq_t)
        for which, colbase, qm_slot in ((0, QP, 4), (1, KP, 5)):
            with nc.Block() as blk:
                @blk.vector
                def _(vsq):
                    ins = vsq.tensor_tensor(ap(sq_t, SQc, 2048), ap(kq_list[which], colbase, 2048),
                                            ap(kq_list[which], colbase, 2048), ALU.mult)
                    P.inc(ins, EV)
                base_mm = P.val(MM)
                @blk.tensor
                def _(t):
                    t.wait_ge(P.S(EV), P.val(EV))
                    for ti in range(4):
                        ins = t.matmul(ps[ti][0:2, :], ap(CSB, E2B, 2), ap(sq_t, SQc + ti * 512, 512),
                                       start=True, stop=True)
                        P.inc(ins, MM)
                @blk.vector
                def _(v):
                    v.wait_ge(P.S(MM), base_mm + 4)
                    for ti in range(4):
                        v.tensor_reduce(ap(SM, S_QM + ti, 1, pn=2), ps[ti][0:2, :], AX.X, ALU.max)
                    v.drain()
                    ins = v.tensor_reduce(ap(SM, S_QM + qm_slot, 1, pn=2), ap(SM, S_QM, 4, pn=2),
                                          AX.X, ALU.max)
                    P.inc(ins, EV)
        with nc.Block() as blk:
            base_mm = P.val(MM)
            @blk.vector
            def _(v):
                v.tensor_tensor(ap(SM, S_QM + 6, 1, pn=2), ap(SM, S_QM + 4, 1, pn=2),
                                ap(SM, S_QM + 5, 1, pn=2), ALU.add)
                v.drain()
                v.tensor_scalar(ap(SM, S_QM + 7, 1, pn=2), ap(SM, S_QM + 6, 1, pn=2),
                                coef, 0.0, ALU.mult, ALU.add)
                v.drain()
                ins = v.tensor_scalar(ap(SM, S_C2, 2, pn=2), IDF[0:2, 0:2], ap(SM, S_QM + 7, 1, pn=2),
                                      1.0, ALU.mult, ALU.mult)
                P.inc(ins, EV)
            @blk.tensor
            def _(t):
                t.wait_ge(P.S(EV), P.val(EV))
                ins = t.matmul(ps[0][:, 0:2], ap(CST, ONES2, 128, pn=2), ap(SM, S_C2, 2, pn=2),
                               start=True, stop=True, tile_position=(0, 0))
                P.inc(ins, MM)
            @blk.vector
            def _(v2):
                v2.wait_ge(P.S(MM), base_mm + 1)
                ins = v2.tensor_scalar(ap(SM, S_NC, 2), ps[0][:, 0:2], -1.0, 0.0, ALU.mult, ALU.add)
                P.inc(ins, EV)

    def emit_attn_pair2(pair, l1, k_cols, q_cols, k_t=None, q_t=None,
                        p_t=None, p_base=10240, t32=None, t32_base=1024,
                        r32=None, r32_base=7168):
        k_t = k_t if k_t is not None else Hb
        q_t = q_t if q_t is not None else Hb
        p_t = p_t if p_t is not None else E
        t32 = t32 if t32 is not None else Ef32
        r32 = r32 if r32 is not None else Ef32
        """scores+exp+AV+normalize for one pair. Writes normalized AV to C[pair*2048 + qt*512]."""
        sched = pair_sched(pair, l1)
        hA, hB = 2 * pair, 2 * pair + 1
        PBUF = p_base
        TSB = t32_base
        RSB = r32_base
        DSBA, DSBB = 0, 512
        with nc.Block() as blk:
            base_mm, base_ac, base_ev, base_dn = P.val(MM), P.val(AC), P.val(EV), P.val("dn")

            @blk.tensor
            def _(t):
                sc_i, mm_i = 0, 0
                for qt in range(NQT):
                    nk = len(sched[qt])
                    for i, ktp in enumerate(sched[qt]):
                        if sc_i >= 1:
                            if l1:
                                t.wait_ge(P.S(EV), base_ev + 2 * sc_i)
                            else:
                                t.wait_ge(P.S(AC), base_ac + 2 * sc_i)
                        for sub in range(2):
                            kt = 2 * ktp + sub
                            ins = t.matmul(ps[0 + sub][:, :], ap(k_t, k_cols + kt * 128, 128, p0=0, pn=64),
                                           ap(q_t, q_cols + qt * 512, 512, p0=0, pn=64),
                                           start=True, stop=True, tile_position=(0, 0))
                            ins = t.matmul(ps[2 + sub][:, :], ap(k_t, k_cols + kt * 128, 128, p0=64, pn=64),
                                           ap(q_t, q_cols + qt * 512, 512, p0=64, pn=64),
                                           start=True, stop=True, tile_position=(64, 0))
                        P.inc(ins, MM)
                        sc_i += 1
                        mm_i += 1
                        t.wait_ge(P.S(AC), base_ac + 2 * sc_i)
                        pb = (sc_i - 1) % 2
                        for sub in range(2):
                            st_ = (i == 0 and sub == 0)
                            sp_ = (i == nk - 1 and sub == 1)
                            ins = t.matmul(ps[4][0:65, :], ap(VT, (2 * ktp + sub) * 80, 65),
                                           ap(p_t, PBUF + pb * 1024 + sub * 512, 512),
                                           start=st_, stop=sp_, tile_position=(0, 0))
                            ins = t.matmul(ps[5][64:128, :], ap(VT, 1280 + (2 * ktp + sub) * 64, 64),
                                           ap(p_t, PBUF + 2048 + pb * 1024 + sub * 512, 512),
                                           start=st_, stop=sp_)
                            ins = t.matmul(ps[5][0:1, :], ap(VT, 64, 1),
                                           ap(p_t, PBUF + 2048 + pb * 1024 + sub * 512, 512),
                                           start=st_, stop=sp_, tile_position=(0, 0))
                        P.inc(ins, MM)
                        mm_i += 1
                    t.wait_ge(P.S("dn"), base_dn + 2 * qt + 1)
                    ins = t.matmul(ps[0][:, :], ONF[64:65, :], ap(H, DSBA, 512, p0=64, pn=1),
                                   start=True, stop=True, tile_position=(64, 0))
                    ins = t.matmul(ps[1][:, :], ONF[0:1, :], ap(H, DSBB, 512, p0=0, pn=1),
                                   start=True, stop=True, tile_position=(0, 0))
                    P.inc(ins, MM)
                    mm_i += 1
                    if qt < NQT - 1:
                        t.wait_ge(P.S("dn"), base_dn + 2 * qt + 2)

            @blk.vector
            def _(v):
                sc_i, mm_i = 0, 0
                for qt in range(NQT):
                    nk = len(sched[qt])
                    for i, ktp in enumerate(sched[qt]):
                        if l1:
                            v.wait_ge(P.S(MM), base_mm + mm_i + 1)
                            pb = sc_i % 2
                            band = 2 * ktp - 4 * qt + 12
                            for hi, h in enumerate((hA, hB)):
                                ins = v.scalar_tensor_tensor(
                                    ap(t32, TSB + (2 * hi + pb) * 1024, 1024),
                                    ap(DM, band * 512, 1024), -8.0 * SLOPES[h],
                                    (pw0, pw1)[hi][:, :], ALU.mult, ALU.add)
                                P.inc(ins, EV)
                        sc_i += 1
                        mm_i += 2
                    mm_i += 1
                    v.wait_ge(P.S(MM), base_mm + mm_i - 1)
                    v.tensor_scalar(ap(H, DSBA, 512, p0=64, pn=1), ps[4][64:65, :], 1e-30, 0.0,
                                    ALU.add, ALU.add)
                    v.tensor_scalar(ap(H, DSBB, 512, p0=0, pn=1), ps[5][0:1, :], 1e-30, 0.0,
                                    ALU.add, ALU.add)
                    v.drain()
                    v.reciprocal(ap(H, DSBA, 512, p0=64, pn=1), ap(H, DSBA, 512, p0=64, pn=1))
                    ins = v.reciprocal(ap(H, DSBB, 512, p0=0, pn=1), ap(H, DSBB, 512, p0=0, pn=1))
                    v.drain()
                    P.inc(ins, "dn")
                    v.wait_ge(P.S(MM), base_mm + mm_i)
                    v.tensor_copy(ap(r32, RSB, 512), ps[0][:, :])
                    v.tensor_copy(ap(r32, RSB + 512, 512), ps[1][:, :])
                    v.drain()
                    v.tensor_tensor(ap(C, pair * 2048 + qt * 512, 512, p0=0, pn=64),
                                    ps[4][0:64, :], ap(r32, RSB, 512, p0=0, pn=64), ALU.mult)
                    ins = v.tensor_tensor(ap(C, pair * 2048 + qt * 512, 512, p0=64, pn=64),
                                          ps[5][64:128, :], ap(r32, RSB + 512, 512, p0=64, pn=64), ALU.mult)
                    v.drain()
                    P.inc(ins, "dn")

            @blk.scalar
            def _(s):
                sc_i, mm_i, ev_i = 0, 0, 0
                for qt in range(NQT):
                    for i, ktp in enumerate(sched[qt]):
                        pb = sc_i % 2
                        if l1:
                            ev_i += 2
                            s.wait_ge(P.S(EV), base_ev + ev_i)
                            s.activation(ap(p_t, PBUF + pb * 1024, 1024), ap(t32, TSB + pb * 1024, 1024),
                                         AF.Exp, scale=0.125, bias=ap(SM, S_NC, 1)).then_inc(P.S(AC), 1)
                            P.cnt[AC] += 1
                            ins = s.activation(ap(p_t, PBUF + 2048 + pb * 1024, 1024),
                                               ap(t32, TSB + (2 + pb) * 1024, 1024),
                                               AF.Exp, scale=0.125, bias=ap(SM, S_NC + 1, 1))
                            P.inc(ins, AC)
                        else:
                            s.wait_ge(P.S(MM), base_mm + mm_i + 1)
                            s.activation(ap(p_t, PBUF + pb * 1024, 1024), pw0[:, :], AF.Exp,
                                         scale=0.125, bias=ap(SM, S_NC, 1)).then_inc(P.S(AC), 1)
                            P.cnt[AC] += 1
                            ins = s.activation(ap(p_t, PBUF + 2048 + pb * 1024, 1024), pw1[:, :],
                                               AF.Exp, scale=0.125, bias=ap(SM, S_NC + 1, 1))
                            P.inc(ins, AC)
                        sc_i += 1
                        mm_i += 2
                    mm_i += 1

    def emit_blockend_stats(ytile):
        with nc.Block() as blk:
            @blk.scalar
            def _(s):
                for ti in range(NT):
                    ins = s.activation(ap(H, 0, 1024), ytile(ti), AF.Square,
                                       accum_out=ap(SM, S_SQ + ti, 1))
                    P.inc(ins, AC)
            @blk.vector
            def _(v):
                for ti in range(NT):
                    ins = v.reduce_sum(ap(SM, S_SUM + ti, 1), ytile(ti), AX.X)
                P.inc(ins, EV)
        with nc.Block() as blk:
            @blk.vector
            def _(v):
                v.tensor_scalar(ap(SM, S_MEAN, 16), ap(SM, S_SUM, 16), 1.0 / D, 0.0, ALU.mult, ALU.add)
                v.drain()
                v.tensor_tensor(ap(SM, S_TMP, 16), ap(SM, S_MEAN, 16), ap(SM, S_MEAN, 16), ALU.mult)
                v.drain()
                v.scalar_tensor_tensor(ap(SM, S_VAR, 16), ap(SM, S_SQ, 16), 1.0 / D, ap(SM, S_TMP, 16),
                                       ALU.mult, ALU.subtract)
                v.drain()
                v.tensor_scalar(ap(SM, S_VAR, 16), ap(SM, S_VAR, 16), EPS, 0.0, ALU.add, ALU.add)
                v.drain()
                ins = v.reciprocal(ap(SM, S_TMP, 16), ap(SM, S_VAR, 16))
                P.inc(ins, EV)
            @blk.scalar
            def _(s):
                s.wait_ge(P.S(EV), P.val(EV))
                ins = s.activation(ap(SM, S_R2, 16), ap(SM, S_TMP, 16), AF.Sqrt)
                P.inc(ins, AC)
        with nc.Block() as blk:
            @blk.vector
            def _(v):
                ins = v.scalar_tensor_tensor(ap(SM, S_NMR2, 16), ap(SM, S_MEAN, 16), -1.0,
                                             ap(SM, S_R2, 16), ALU.mult, ALU.mult)
                P.inc(ins, EV)

    def emit_unify_blockend(wsb, wbase, bias_sb, bias_base, x_src, first=False, out_dst=None):
        with nc.Block() as blk:
            base_mm, base_ev = P.val(MM), P.val(EV)
            @blk.tensor
            def _(t):
                for ti in range(NT):
                    for dt_ in range(2):
                        grp = 2 * ti + dt_
                        slot = grp % 6
                        if grp >= 6:
                            t.wait_ge(P.S(EV), base_ev + grp - 5)
                        for k in range(NCH):
                            ins = t.matmul(ps[slot][:, :],
                                           ap(C, k * 2048 + ti * 128, 128),
                                           ap(wsb, wbase + k * 1024 + dt_ * 512, 512),
                                           start=(k == 0), stop=(k == NCH - 1))
                        P.inc(ins, MM)
            @blk.vector
            def _(v):
                for ti in range(NT):
                    for dt_ in range(2):
                        grp = 2 * ti + dt_
                        slot = grp % 6
                        v.wait_ge(P.S(MM), base_mm + grp + 1)
                        ins = v.tensor_tensor(ap(G, ti * 1024 + dt_ * 512, 512), ps[slot][:, :],
                                              ap(bias_sb, bias_base + dt_ * 512, 512), ALU.add)
                        P.inc(ins, EV)
            base_ev2 = P.val(EV) - 2 * NT
            @blk.scalar
            def _(s):
                for ti in range(NT):
                    s.wait_ge(P.S(EV), base_ev2 + 2 * ti + 2)
                    ins = s.activation(ap(H, 0, 1024), ap(G, ti * 1024, 1024), AF.Square,
                                       accum_out=ap(SM, S_SQ + ti, 1))
                    P.inc(ins, AC)
            @blk.vector
            def _(v):
                for ti in range(NT):
                    v.wait_ge(P.S(EV), base_ev2 + 2 * ti + 2)
                    ins = v.reduce_sum(ap(SM, S_SUM + ti, 1), ap(G, ti * 1024, 1024), AX.X)
                P.inc(ins, EV)
        with nc.Block() as blk:
            @blk.vector
            def _(v):
                v.wait_ge(P.S(AC), P.val(AC))
                v.tensor_scalar(ap(SM, S_MEAN, 16), ap(SM, S_SUM, 16), 1.0 / D, 0.0, ALU.mult, ALU.add)
                v.drain()
                v.tensor_tensor(ap(SM, S_TMP, 16), ap(SM, S_MEAN, 16), ap(SM, S_MEAN, 16), ALU.mult)
                v.drain()
                v.scalar_tensor_tensor(ap(SM, S_VAR, 16), ap(SM, S_SQ, 16), 1.0 / D, ap(SM, S_TMP, 16),
                                       ALU.mult, ALU.subtract)
                v.drain()
                v.tensor_scalar(ap(SM, S_VAR, 16), ap(SM, S_VAR, 16), EPS, 0.0, ALU.add, ALU.add)
                v.drain()
                ins = v.reciprocal(ap(SM, S_TMP, 16), ap(SM, S_VAR, 16))
                P.inc(ins, EV)
            @blk.scalar
            def _(s):
                s.wait_ge(P.S(EV), P.val(EV))
                ins = s.activation(ap(SM, S_R2, 16), ap(SM, S_TMP, 16), AF.Sqrt)
                P.inc(ins, AC)
        with nc.Block() as blk:
            @blk.vector
            def _(v):
                ins = v.scalar_tensor_tensor(ap(SM, S_NMR2, 16), ap(SM, S_MEAN, 16), -1.0,
                                             ap(SM, S_R2, 16), ALU.mult, ALU.mult)
                P.inc(ins, EV)
        emit_blockend_pass2(lambda ti: ap(G, ti * 1024, 1024), x_src, out_dst,
                            bufs=(Ef32, 1024))

    def emit_blockend_pass2(ytile, x_src, out_dst, bufs):
        buf_t, buf_c = bufs
        dst_dram = out_d if out_dst == "out" else x_state
        with nc.Block() as blk:
            base_sy = P.val(SY)
            base_ev = P.val(EV)
            @blk.sync
            def _(sy):
                for ti in range(NT):
                    if ti >= 2:
                        sy.wait_ge(P.S(EV), base_ev + ti - 1)
                    P.inc(sy.dma_start(ap(buf_t, buf_c + (ti % 2) * 1024, 1024),
                                       x_src[ti * 128:(ti + 1) * 128, :]), SY, 16)
            @blk.scalar
            def _(s):
                for ti in range(NT):
                    if ti >= 2:
                        s.wait_ge(P.S(EV), base_ev + ti - 1)
                    ins = s.activation(ap(buf_t, buf_c + 2048 + (ti % 2) * 1024, 1024), ytile(ti),
                                       AF.Identity, bias=ap(SM, S_NMR2 + ti, 1),
                                       scale=ap(SM, S_R2 + ti, 1))
                    P.inc(ins, AC)
            base_ac = P.val(AC) - NT
            @blk.vector
            def _(v):
                for ti in range(NT):
                    v.wait_ge(P.S(AC), base_ac + ti + 1)
                    v.wait_ge(P.S(SY), base_sy + 16 * (ti + 1))
                    ins = v.tensor_tensor(ytile(ti), ap(buf_t, buf_c + 2048 + (ti % 2) * 1024, 1024),
                                          ap(buf_t, buf_c + (ti % 2) * 1024, 1024), ALU.add)
                    P.inc(ins, EV)
            @blk.gpsimd
            def _(g):
                for ti in range(NT):
                    g.wait_ge(P.S(EV), base_ev + ti + 1)
                    P.inc(g.dma_start(dst_dram[ti * 128:(ti + 1) * 128, :], ytile(ti)), ST, 16)
                g.wait_ge(P.S(ST), P.val(ST))


    def emit_l0_ffn(out_dst=None):
        emit_ln(x_state)
        emit_transpose_tm_fm(C, 0, 1024, A, 0, TOK, NT, NCH)
        WG1C = 4096
        with nc.Block() as blk:
            @blk.gpsimd
            def _(g):
                for kk in range(NCH):
                    P.inc(g.dma_start(ap(Hb, WG1C + kk * 256, 256), wg1[kk * 128:(kk + 1) * 128, :]), GQ, 16)
                g.wait_ge(P.S(GQ), P.val(GQ))
        emit_proj_fm(Hb, WG1C, A, 0, Hb, 0, 2, act=AF.Gelu, bias_cols=CST, bias_base=BG1C,
                     kstride=256)
        with nc.Block() as blk:
            base_mm, base_ev, base_ac0 = P.val(MM), P.val(EV), P.val(AC)
            @blk.tensor
            def _(t):
                for ti in range(NT):
                    for kk in range(2):
                        ins = t.matmul(pw0[:, ti * 8:ti * 8 + 8], ap(Hb, kk * 2048 + ti * 128, 128),
                                       ap(CSB, WG2B + kk * 8, 8), start=(kk == 0), stop=(kk == 1))
                P.inc(ins, MM)
            @blk.vector
            def _(v):
                v.wait_ge(P.S(MM), base_mm + 1)
                ins = v.tensor_copy(ap(H, 0, 128), pw0[:, 0:128])
                P.inc(ins, EV)
            @blk.scalar
            def _(s):
                s.wait_ge(P.S(EV), base_ev + 1)
                ins = s.activation(ap(H, 0, 128), ap(H, 0, 128), AF.Exp)
                P.inc(ins, AC)
            @blk.vector
            def _(v2):
                v2.wait_ge(P.S(AC), base_ac0 + 1)
                v2.tensor_reduce(ap(H, 128, 16), bass.AP(H, 0, [[H.shape[1], 128], [8, 16], [1, 8]]),
                                 AX.X, ALU.add)
                v2.drain()
                v2.reciprocal(ap(H, 144, 16), ap(H, 128, 16))
                v2.drain()
                v2.tensor_tensor(ap(H, 0, 128), ap(H, 0, 128),
                                 bass.AP(H, 144, [[H.shape[1], 128], [1, 16], [0, 8]]), ALU.mult)
                v2.drain()
                for ti in range(NT):
                    v2.max(ap(H, 176 + ti * 8, 8), ap(H, ti * 8, 8))
                v2.drain()
                v2.tensor_copy(ap(H, 160, 16), bass.AP(H, 177, [[H.shape[1], 128], [8, 16]]))
                v2.drain()
                v2.tensor_tensor(SMI[:, 0:128], ap(H, 0, 128),
                                 bass.AP(H, 160, [[H.shape[1], 128], [1, 16], [0, 8]]), ALU.is_ge)
                v2.drain()
                v2.memset(ap(H, 304, 8), 0.0)
                v2.drain()
                ins = v2.select(ap(SM, S_G, 128), SMI[:, 0:128], ap(H, 0, 128),
                                bass.AP(H, 304, [[H.shape[1], 128], [0, 16], [1, 8]]))
                P.inc(ins, EV)
        if "gates" in tapd:
            emit_tap(tapd["gates"], SM, S_G, NT * 8)

        WQ0, WQ1 = 0, 4096
        def woslot(j):
            return (Hb, j * 1024) if j < 6 else (VT, (j - 6) * 1024)
        WI, WO = "wi", "wo"
        with nc.Block() as blk:
            base_mm_e, base_ac_e, base_ev_e = P.val(MM), P.val(AC), P.val(EV)
            base_wi, base_wo = P.val(WI), P.val(WO)
            def h_done(e, q):
                return base_mm_e + e * 80 + (q + 1) * 4
            def o_done(e, half):
                return base_mm_e + e * 80 + 16 + (half + 1) * 32

            @blk.gpsimd
            def _(g):
                def win(e, q):
                    fbase = (WQ0, WQ1)[(e * 4 + q) % 2]
                    for kk in range(NCH):
                        P.inc(g.dma_start(ap(Fb, fbase + kk * 512, 512),
                                          bass.AP(w_ine, e * D * 2048 + kk * 128 * 2048 + q * 512,
                                                  [[2048, 128], [1, 512]])), WI, 16)
                def wout(e, half):
                    for j in range(8):
                        hh = half * 8 + j
                        tt_, cc = woslot(j)
                        P.inc(g.dma_start(ap(tt_, cc, 1024),
                                          bass.AP(w_oute, (e * 2048 + hh * 128) * D,
                                                  [[D, 128], [1, 1024]])),
                              WO, 16)
                win(0, 0); win(0, 1)
                for e in range(8):
                    g.wait_ge(P.S(MM), h_done(e, 0))
                    win(e, 2)
                    g.wait_ge(P.S(MM), h_done(e, 1))
                    win(e, 3)
                    if e > 0:
                        g.wait_ge(P.S(MM), o_done(e - 1, 1))
                    wout(e, 0)
                    if e < 7:
                        g.wait_ge(P.S(MM), h_done(e, 2))
                        win(e + 1, 0)
                    g.wait_ge(P.S(MM), o_done(e, 0))
                    wout(e, 1)
                    if e < 7:
                        g.wait_ge(P.S(MM), h_done(e, 3))
                        win(e + 1, 1)

            @blk.tensor
            def _(t):
                for e in range(8):
                    for q in range(4):
                        gq = e * 4 + q
                        fbase = (WQ0, WQ1)[gq % 2]
                        t.wait_ge(P.S(WI), base_wi + (gq + 1) * 128)
                        for hq in range(4):
                            for tt_ in range(4):
                                gidx = gq * 16 + hq * 4 + tt_
                                gslot = gidx % 6
                                if gidx >= 6:
                                    t.wait_ge(P.S(AC), base_ac_e + gidx - 5)
                                for kk in range(NCH):
                                    ins = t.matmul(ps[gslot][:, :],
                                                   ap(Fb, fbase + kk * 512 + hq * 128, 128),
                                                   ap(A, kk * 2048 + tt_ * 512, 512),
                                                   start=(kk == 0), stop=(kk == NCH - 1))
                            ins.then_inc(P.S(MM), 1)
                            P.cnt[MM] += 1
                    t.wait_ge(P.S(AC), base_ac_e + (e + 1) * 64)
                    for half in range(2):
                        t.wait_ge(P.S(WO), base_wo + (e * 2 + half + 1) * 128)
                        for ti in range(NT):
                            for dt_ in range(2):
                                gidx = half * 32 + ti * 2 + dt_
                                gslot = gidx % 6
                                if e * 64 + gidx >= 6:
                                    t.wait_ge(P.S(EV), base_ev_e + e * 64 + gidx - 5)
                                for j in range(8):
                                    hh = half * 8 + j
                                    wt, wc = woslot(j)
                                    harena, hcol = (C, hh * 2048) if hh < 8 else (Dd, (hh - 8) * 2048)
                                    ins = t.matmul(ps[gslot][:, :],
                                                   ap(harena, hcol + ti * 128, 128),
                                                   ap(wt, wc + dt_ * 512, 512),
                                                   start=(j == 0), stop=(j == 7))
                                ins.then_inc(P.S(MM), 1)
                                P.cnt[MM] += 1

            @blk.scalar
            def _(s):
                for e in range(8):
                    if e > 0:
                        s.wait_ge(P.S(MM), o_done(e - 1, 1))
                    for q in range(4):
                        gq = e * 4 + q
                        for hq in range(4):
                            H_ = q * 4 + hq
                            s.wait_ge(P.S(MM), base_mm_e + e * 80 + q * 4 + hq + 1)
                            for tt_ in range(4):
                                gidx = gq * 16 + hq * 4 + tt_
                                gslot = gidx % 6
                                harena, hcol = (C, H_ * 2048) if H_ < 8 else (Dd, (H_ - 8) * 2048)
                                ins = s.activation(ap(harena, hcol + tt_ * 512, 512), ps[gslot][:, :],
                                                   AF.Gelu)
                                ins.then_inc(P.S(AC), 1)
                                P.cnt[AC] += 1

            @blk.vector
            def _(v):
                for e in range(8):
                    for half in range(2):
                        for ti in range(NT):
                            for dt_ in range(2):
                                gidx = half * 32 + ti * 2 + dt_
                                gslot = gidx % 6
                                v.wait_ge(P.S(MM), base_mm_e + e * 80 + 16 + gidx + 1)
                                yt, yc = (YS0, ti * 1024) if ti < 8 else (YS1, (ti - 8) * 1024)
                                gate = ap(SM, S_G + ti * 8 + e, 1)
                                if e == 0 and half == 0:
                                    ins = v.tensor_scalar(ap(yt, yc + dt_ * 512, 512), ps[gslot][:, :],
                                                          gate, 0.0, ALU.mult, ALU.add)
                                else:
                                    ins = v.scalar_tensor_tensor(ap(yt, yc + dt_ * 512, 512),
                                                                 ps[gslot][:, :], gate,
                                                                 ap(yt, yc + dt_ * 512, 512),
                                                                 ALU.mult, ALU.add)
                                ins.then_inc(P.S(EV), 1)
                                P.cnt[EV] += 1
        def ytile_l0f(ti):
            return ap(YS0, ti * 1024, 1024) if ti < 8 else ap(YS1, (ti - 8) * 1024, 1024)
        emit_blockend_stats(ytile_l0f)
        emit_blockend_pass2(ytile_l0f, x_state, out_dst, bufs=(G, 0))


    def emit_l1_attn(out_dst=None):
        emit_ln(x_state)
        emit_transpose_tm_fm(C, 0, 1024, A, 0, TOK, NT, NCH)
        with nc.Block() as blk:
            @blk.sync
            def _(sy):
                P.inc(sy.dma_start(ap(H, 0, 1024, pn=1), bout.ap()[None, :]), SY, 16)
        with nc.Block() as blk:
            @blk.tensor
            def _(t):
                t.wait_ge(P.S(SY), P.val(SY))
                for dt_ in range(2):
                    ins = t.matmul(ps[dt_][:, :], ONF[0:1, :], ap(H, dt_ * 512, 512, pn=1),
                                   start=True, stop=True, tile_position=(0, 0))
                P.inc(ins, MM)
            @blk.vector
            def _(v):
                v.wait_ge(P.S(MM), P.val(MM))
                for dt_ in range(2):
                    v.tensor_copy(ap(CSB, BUNIB + dt_ * 512, 512), ps[dt_][:, :])
        WV = "wv"
        base_wv = P.val(WV)
        with nc.Block() as blk:
            base_ev = P.val(EV)
            @blk.gpsimd
            def _(g):
                for pair in range(NPAIR):
                    if pair >= 2:
                        g.wait_ge(P.S(EV), base_ev + (pair - 1) * 12)
                    fbase = (pair % 2) * 3072
                    for pi in range(3):
                        src_ap = bass.AP(wqkv, pi * 1024 + pair * 128,
                                         [[3072, 128], [128 * 3072, 8], [1, 128]])
                        P.inc(g.dma_start(ap(Fb, fbase + pi * 1024, 1024), src_ap), WV, 16)
            @blk.tensor
            def _(t):
                for pair in range(NPAIR):
                    fbase = (pair % 2) * 3072
                    t.wait_ge(P.S(WV), base_wv + (pair + 1) * 48)
                    for pi in range(3):
                        for ti in range(4):
                            grp = pair * 12 + pi * 4 + ti
                            slot = grp % 6
                            if grp >= 6:
                                t.wait_ge(P.S(EV), base_ev + grp - 5)
                            for kk in range(NCH):
                                ins = t.matmul(ps[slot][:, :],
                                               ap(Fb, fbase + pi * 1024 + kk * 128, 128),
                                               ap(A, kk * 2048 + ti * 512, 512),
                                               start=(kk == 0), stop=(kk == NCH - 1))
                            P.inc(ins, MM)
            base_mm2 = P.val(MM) - NPAIR * 12
            @blk.scalar
            def _(s):
                for pair in range(NPAIR):
                    for pi in range(3):
                        dst_t = (Bb, Dd, E)[pi] if True else None
                        for ti in range(4):
                            grp = pair * 12 + pi * 4 + ti
                            slot = grp % 6
                            s.wait_ge(P.S(MM), base_mm2 + grp + 1)
                            ins = s.activation(ap(dst_t, pair * 2048 + ti * 512, 512), ps[slot][:, :],
                                               AF.Identity, bias=ap(CST, BQKV + pi * 8 + pair, 1))
                            ins.then_inc(P.S(EV), 1)
                            P.cnt[EV] += 1
        with nc.Block() as blk:
            @blk.sync
            def _(sy):
                P.inc(sy.dma_start(DM[:, :], dmast[:, :]), SY, 16)
                sy.wait_ge(P.S(SY), P.val(SY))
        emit_vt_init()
        for pair in range(NPAIR):
            emit_v_transpose(E, pair * 2048)
            emit_negC(pair * 2048, pair * 2048, 4096, 0.0625, (Bb, Dd), sq_t=Hb)
            if pair == 0 and "sm1" in tapd:
                emit_tap(tapd["sm1"], SM, 0, 384)
            emit_attn_pair2(pair, l1=True, k_cols=pair * 2048, q_cols=pair * 2048, k_t=Dd, q_t=Bb,
                            p_t=Hb, p_base=0, t32=Ff32, t32_base=0, r32=G, r32_base=7168)
            if pair == 0 and "t_dbg" in tapd:
                with nc.Block() as blk2:
                    @blk2.gpsimd
                    def _(g):
                        P.inc(g.dma_start(tapd["t_dbg"][:, :],
                                          bass.AP(Ff32, 0, [[4096, 128], [1, 4096]])), ST, 16)
                        g.wait_ge(P.S(ST), P.val(ST))
        if "av_fm" in tapd:
            emit_tap(tapd["av_fm"], C, 0, NCH * TOK)
        with nc.Block() as blk:
            @blk.gpsimd
            def _(g):
                for kk in range(NCH):
                    P.inc(g.dma_start(ap(Fb, kk * 1024, 1024), wout_d[kk * 128:(kk + 1) * 128, :]), GQ, 16)
                g.wait_ge(P.S(GQ), P.val(GQ))
        emit_unify_blockend(Fb, 0, CSB, BUNIB, x_state, out_dst=out_dst)


    def hs_tile(ti):
        arena = (C, Dd, E, Bb)[ti // 4]
        return arena, (ti % 4) * 4096

    def hs_f32(ti):
        arena = (CF32, DF32, YS1, YS0)[ti // 4]
        return arena, (ti % 4) * 2048

    def emit_wstats(wdram, nrow_chunks, row_w, dst_scol):
        nhalf = max(1, row_w // 2048)
        cw_ = min(row_w, 2048)
        n = nrow_chunks * nhalf
        with nc.Block() as blk:
            base_sy, base_ev = P.val(SY), P.val(EV)
            @blk.sync
            def _(sy):
                for i in range(n):
                    if i >= 2:
                        sy.wait_ge(P.S(EV), base_ev + i - 1)
                    ch, hf = divmod(i, nhalf)
                    P.inc(sy.dma_start(ap(Ff32, (i % 2) * 2048, cw_),
                                       bass.AP(wdram, ch * 128 * row_w + hf * 2048,
                                               [[row_w, 128], [1, cw_]])), SY, 16)
            @blk.vector
            def _(v):
                for i in range(n):
                    v.wait_ge(P.S(SY), base_sy + 16 * (i + 1))
                    v.tensor_reduce(ap(SM, S_WS + 5, 1), ap(Ff32, (i % 2) * 2048, cw_), AX.X,
                                    ALU.add, apply_absolute_value=True)
                    v.drain()
                    if i == 0:
                        ins = v.tensor_copy(ap(SM, S_WS + 4, 1), ap(SM, S_WS + 5, 1))
                    else:
                        ins = v.tensor_tensor(ap(SM, S_WS + 4, 1), ap(SM, S_WS + 4, 1),
                                              ap(SM, S_WS + 5, 1), ALU.add)
                    v.drain()
                    P.inc(ins, EV)
        with nc.Block() as blk:
            base_mm, base_ev = P.val(MM), P.val(EV)
            @blk.tensor
            def _(t):
                ins = t.matmul(ps[0][0:1, 0:1], ap(SM, S_WS + 4, 1), ONF[:, 0:1],
                               start=True, stop=True)
                P.inc(ins, MM)
                t.wait_ge(P.S(EV), base_ev + 1)
                t.matmul(ps[1][:, 0:1], ONF[0:1, :], ap(SM, S_WS + 6, 1, pn=1),
                         start=True, stop=True, tile_position=(0, 0))
                ins = t.matmul(ps[2][:, 0:1], ONF[0:1, :], ap(SM, S_WS + 5, 1, pn=1),
                               start=True, stop=True, tile_position=(0, 0))
                P.inc(ins, MM)
            @blk.vector
            def _(v):
                v.wait_ge(P.S(MM), base_mm + 1)
                nelem = float(nrow_chunks * 128 * row_w)
                v.tensor_scalar(ap(SM, S_WS + 5, 1, pn=1), ps[0][0:1, 0:1], 1.0 / nelem, 1e-5,
                                ALU.mult, ALU.max)
                v.drain()
                ins = v.reciprocal(ap(SM, S_WS + 6, 1, pn=1), ap(SM, S_WS + 5, 1, pn=1))
                v.drain()
                P.inc(ins, EV)
                v.wait_ge(P.S(MM), base_mm + 2)
                v.tensor_copy(ap(SM, dst_scol, 1), ps[1][:, 0:1])
                ins = v.tensor_copy(ap(SM, dst_scol + 1, 1), ps[2][:, 0:1])
                P.inc(ins, EV)

    def emit_w2_prep():
        for r in range(8):
            with nc.Block() as blk:
                base_sy, base_ev, base_mm = P.val(SY), P.val(EV), P.val(MM)
                @blk.sync
                def _(sy):
                    for hf in range(2):
                        P.inc(sy.dma_start(ap(Ff32, hf * 2048, 2048),
                                           bass.AP(w2d, r * 128 * 4096 + hf * 2048,
                                                   [[4096, 128], [1, 2048]])), SY, 16)
                @blk.vector
                def _(v):
                    for quarter in range(4):
                        v.wait_ge(P.S(SY), base_sy + 16 * (quarter // 2 + 1))
                        v.tensor_scalar(ap(H, 0, 1024), ap(Ff32, quarter * 1024, 1024),
                                        ap(SM, S_WS + 2, 1), MAGIC, ALU.mult, ALU.add)
                        v.drain()
                        v.tensor_scalar(ap(H, 0, 1024), ap(H, 0, 1024), MAGIC, 1.0,
                                        ALU.subtract, ALU.min)
                        v.drain()
                        ins = v.tensor_scalar(ap(Hb, quarter * 1024, 1024), ap(H, 0, 1024), -1.0, 0.0,
                                              ALU.max, ALU.add)
                        v.drain()
                        P.inc(ins, EV)
                @blk.tensor
                def _(t):
                    for b in range(4):
                        t.wait_ge(P.S(EV), base_ev + 2 * b // 2 + 1)
                        pt = (pt0, pt1)[b % 2]
                        if b >= 2:
                            t.wait_ge(P.S(EV), base_ev + 4 + (b - 2) + 1)
                        for s8 in range(8):
                            ss = b * 8 + s8
                            ins = t.transpose(pt[:, s8 * 128:(s8 + 1) * 128],
                                              ap(Hb, ss * 128, 128), ap(CSB, IDB, 128))
                        ins.then_inc(P.S(MM), 1)
                        P.cnt[MM] += 1
                @blk.vector
                def _(v2):
                    base_st = P.val(ST)
                    for b in range(4):
                        v2.wait_ge(P.S(MM), base_mm + b + 1)
                        if b >= 2:
                            v2.wait_ge(P.S(ST), base_st + (b - 1) * 16)
                        pt = (pt0, pt1)[b % 2]
                        ins = v2.tensor_copy(ap(Hb, 4096 + (b % 2) * 1024, 1024), pt[:, :])
                        P.inc(ins, EV)
                @blk.gpsimd
                def _(g):
                    for b in range(4):
                        g.wait_ge(P.S(EV), base_ev + 4 + b + 1)
                        dst = bass.AP(w2t_spill, (b * 8 * 128) * 1024 + r * 128,
                                      [[1024, 128], [128 * 1024, 8], [1, 128]])
                        srcs = bass.AP(Hb, 4096 + (b % 2) * 1024,
                                       [[Hb.shape[1], 128], [128, 8], [1, 128]])
                        P.inc(g.dma_start(dst, srcs), ST, 16)
                    g.wait_ge(P.S(ST), P.val(ST))

    def emit_l1_ffn(out_dst="out"):
        emit_ln(x_state, quant=True)
        emit_transpose_tm_fm(C, 0, 1024, A, 0, TOK, NT, NCH)
        if "xqA" in tapd:
            emit_tap(tapd["xqA"], A, 0, 16384)
        emit_wstats(w1d, 32, 1024, S_WS + 0)
        emit_wstats(w2d, 8, 4096, S_WS + 2)
        if "smw" in tapd:
            emit_tap(tapd["smw"], SM, 0, 384)
        emit_w2_prep()
        if "hbq" in tapd:
            emit_tap(tapd["hbq"], Hb, 0, 6144)
        if "ff" in tapd:
            emit_tap(tapd["ff"], Ff32, 0, 4096)
        HRW, SIL, WQB = 0, 2050, 4098
        WT = 4096
        with nc.Block() as blk:
            @blk.gpsimd
            def _(g):
                g.memset(ap(Hb, HRW, 1), 0.0)
                g.memset(ap(Hb, HRW + 2049, 1), 0.0)
            @blk.sync
            def _(sy):
                P.inc(sy.dma_start(ap(Ff32, 0, 1024),
                                   bass.AP(w1d, 0, [[1024, 128], [1, 1024]])), SY, 16)
        with nc.Block() as blk:
            base_sy, base_ev, base_mm, base_ac = P.val(SY), P.val(EV), P.val(MM), P.val(AC)
            def evq(hc):   return base_ev + hc * 6 + 1
            def evw(hc):   return base_ev + hc * 6 + 2
            def evhs(hc):  return base_ev + hc * 6 + 6
            def mmw(hc):   return base_mm + hc * 7 + 1
            def mmh(hc, t): return base_mm + hc * 7 + 2 + t
            def mms(hc, b): return base_mm + hc * 7 + 6 + b
            def ach(hc, t): return base_ac + hc * 6 + 1 + t
            def acs(hc, q): return base_ac + hc * 6 + 5 + q
            @blk.sync
            def _(sy):
                for hc in range(32):
                    if hc + 1 < 32:
                        if hc >= 1:
                            sy.wait_ge(P.S(EV), evq(hc - 1))
                        P.inc(sy.dma_start(ap(Ff32, ((hc + 1) % 2) * 1024, 1024),
                                           bass.AP(w1d, (hc + 1) * 128 * 1024,
                                                   [[1024, 128], [1, 1024]])), SY, 16)
            @blk.vector
            def _(v):
                for hc in range(32):
                    v.wait_ge(P.S(SY), base_sy + 16 * hc)
                    if hc >= 1:
                        v.wait_ge(P.S(MM), mmw(hc - 1))
                        v.wait_ge(P.S(AC), acs(hc - 1, 1))
                    v.tensor_scalar(ap(H, 0, 1024), ap(Ff32, (hc % 2) * 1024, 1024),
                                    ap(SM, S_WS + 0, 1), MAGIC, ALU.mult, ALU.add)
                    v.drain()
                    v.tensor_scalar(ap(H, 0, 1024), ap(H, 0, 1024), MAGIC, 1.0,
                                    ALU.subtract, ALU.min)
                    v.drain()
                    ins = v.tensor_scalar(ap(Hb, WQB, 1024), ap(H, 0, 1024), -1.0, 0.0,
                                          ALU.max, ALU.add)
                    v.drain()
                    P.inc(ins, EV)
                    v.wait_ge(P.S(MM), mmw(hc))
                    pt = (pt0, pt1)[(hc * 3) % 2]
                    ins = v.tensor_copy(ap(Fb, WT + (hc % 2) * 1024, 1024), pt[:, 0:1024])
                    P.inc(ins, EV)
                    for sub in range(2):
                        v.wait_ge(P.S(AC), ach(hc, min(2 * sub + 2, 3)))
                        if sub >= 1:
                            v.wait_ge(P.S(AC), acs(hc, sub - 1))
                        elif hc >= 1:
                            v.wait_ge(P.S(AC), acs(hc - 1, 1))
                        v.tensor_scalar(ap(H, 0, 1024), ap(Hb, HRW + sub * 1024, 1024),
                                        ap(CST, CONVW + hc * 3, 1), 0.0, ALU.mult, ALU.add)
                        v.drain()
                        v.scalar_tensor_tensor(ap(H, 0, 1024), ap(Hb, HRW + 1 + sub * 1024, 1024),
                                               ap(CST, CONVW + hc * 3 + 1, 1), ap(H, 0, 1024),
                                               ALU.mult, ALU.add)
                        v.drain()
                        ins = v.scalar_tensor_tensor(ap(H, 0, 1024), ap(Hb, HRW + 2 + sub * 1024, 1024),
                                                     ap(CST, CONVW + hc * 3 + 2, 1), ap(H, 0, 1024),
                                                     ALU.mult, ALU.add)
                        v.drain()
                        ins.then_inc(P.S("cv"), 1)
                        P.cnt["cv"] += 1
                    for b in range(2):
                        v.wait_ge(P.S(MM), mms(hc, b))
                        pt = (pt0, pt1)[(hc * 3 + 1 + b) % 2]
                        for half in range(2):
                            ti0 = b * 8 + half * 4
                            arena, colb = hs_tile(ti0)
                            dst = bass.AP(arena, colb + hc * 128,
                                          [[arena.shape[1], 128], [4096, 4], [1, 128]])
                            srcp = bass.AP(pt, half * 512, [[1024, 128], [128, 4], [1, 128]])
                            ins = v.tensor_copy(dst, srcp)
                            P.inc(ins, EV)
            @blk.tensor
            def _(t):
                P.S("cv")
                base_cv = 0
                for hc in range(32):
                    t.wait_ge(P.S(EV), evq(hc))
                    pt = (pt0, pt1)[(hc * 3) % 2]
                    if hc >= 1:
                        t.wait_ge(P.S(EV), evw(hc - 1) + 4)
                    for k in range(8):
                        ins = t.transpose(pt[:, k * 128:(k + 1) * 128], ap(Hb, WQB + k * 128, 128),
                                          ap(CSB, IDB, 128))
                    ins.then_inc(P.S(MM), 1)
                    P.cnt[MM] += 1
                    t.wait_ge(P.S(EV), evw(hc))
                    for tt_ in range(4):
                        slot = (hc * 4 + tt_) % 6
                        if hc * 4 + tt_ >= 6:
                            prev = hc * 4 + tt_ - 6
                            t.wait_ge(P.S(AC), ach(prev // 4, prev % 4))
                        for k in range(8):
                            ins = t.matmul(ps[slot][:, :], ap(Fb, WT + (hc % 2) * 1024 + k * 128, 128),
                                           ap(A, k * 2048 + tt_ * 512, 512),
                                           start=(k == 0), stop=(k == 7))
                        ins.then_inc(P.S(MM), 1)
                        P.cnt[MM] += 1
                    for b in range(2):
                        pt = (pt0, pt1)[(hc * 3 + 1 + b) % 2]
                        t.wait_ge(P.S(AC), acs(hc, b))
                        if b == 0:
                            if hc >= 1:
                                t.wait_ge(P.S(EV), base_ev + 6 * hc)
                        else:
                            t.wait_ge(P.S(EV), base_ev + 6 * hc + 2)
                        for s8 in range(8):
                            ti = b * 8 + s8
                            ins = t.transpose(pt[:, s8 * 128:(s8 + 1) * 128],
                                              ap(Hb, SIL + ti * 128, 128), ap(CSB, IDB, 128))
                        ins.then_inc(P.S(MM), 1)
                        P.cnt[MM] += 1
            @blk.scalar
            def _(s):
                for hc in range(32):
                    for tt_ in range(4):
                        slot = (hc * 4 + tt_) % 6
                        if tt_ == 0 and hc >= 1:
                            s.wait_ge(P.S("cv"), 2 * hc)
                        s.wait_ge(P.S(MM), mmh(hc, tt_))
                        ins = s.activation(ap(Hb, HRW + 1 + tt_ * 512, 512), ps[slot][:, :],
                                           AF.Identity, bias=0.0, scale=ap(SM, S_WS + 1, 1))
                        ins.then_inc(P.S(AC), 1)
                        P.cnt[AC] += 1
                    for sub in range(2):
                        s.wait_ge(P.S("cv"), hc * 2 + sub + 1)
                        ins = s.activation(ap(Hb, SIL + sub * 1024, 1024), ap(H, 0, 1024), AF.Silu,
                                           bias=ap(CST, CONVB + hc, 1))
                        ins.then_inc(P.S(AC), 1)
                        P.cnt[AC] += 1
        if "hsC" in tapd:
            emit_tap(tapd["hsC"], C, 0, 16384)
        with nc.Block() as blk:
            @blk.vector
            def _(v):
                for ti in range(NT):
                    arena, colb = hs_tile(ti)
                    v.tensor_reduce(ap(SM, S_TMP + ti, 1), ap(arena, colb, 4096), AX.X,
                                    ALU.max, apply_absolute_value=True)
                v.drain()
                v.tensor_scalar(ap(SM, S_TMP, 16), ap(SM, S_TMP, 16), 1e-5, 0.0, ALU.max, ALU.add)
                v.drain()
                v.tensor_scalar(ap(SM, S_SCL, 16), ap(SM, S_TMP, 16), 1.0 / 127.0, 0.0,
                                ALU.mult, ALU.add)
                v.drain()
                v.reciprocal(ap(SM, S_ST2, 16), ap(SM, S_SCL, 16))
                v.drain()
                for ti in range(NT):
                    arena, colb = hs_tile(ti)
                    v.tensor_scalar(ap(Ff32, 0, 2048), ap(arena, colb, 2048),
                                    ap(SM, S_ST2 + ti, 1), MAGIC, ALU.mult, ALU.add)
                    v.tensor_scalar(ap(Ff32, 2048, 2048), ap(arena, colb + 2048, 2048),
                                    ap(SM, S_ST2 + ti, 1), MAGIC, ALU.mult, ALU.add)
                    v.drain()
                    v.tensor_scalar(ap(Ff32, 0, 2048), ap(Ff32, 0, 2048), MAGIC, 0.0,
                                    ALU.subtract, ALU.add)
                    v.tensor_scalar(ap(Ff32, 2048, 2048), ap(Ff32, 2048, 2048), MAGIC, 0.0,
                                    ALU.subtract, ALU.add)
                    v.drain()
                    v.tensor_scalar(ap(arena, colb, 2048), ap(Ff32, 0, 2048),
                                    ap(SM, S_SCL + ti, 1), 1.0, ALU.mult, ALU.mult)
                    v.tensor_scalar(ap(arena, colb + 2048, 2048), ap(Ff32, 2048, 2048),
                                    ap(SM, S_SCL + ti, 1), 1.0, ALU.mult, ALU.mult)
                    v.drain()
        if "xq2C" in tapd:
            emit_tap(tapd["xq2C"], C, 0, 16384)
        groups = [(0, 1, 2), (3, 4, 5), (6, 7, 8), (9, 10, 11), (12, 13, 14), (15,)]
        dst_dram = out_d if out_dst == "out" else x_state
        for grp in groups:
            ng = len(grp)
            with nc.Block() as blk:
                base_mm, base_ev, base_ac, base_sy = P.val(MM), P.val(EV), P.val(AC), P.val(SY)
                W2T = "w2t"
                base_w2t = P.val(W2T)
                @blk.tensor
                def _(t):
                    for gi, ti in enumerate(grp):
                        arena, colb = hs_tile(ti)
                        for b in range(4):
                            idx = gi * 4 + b
                            pt = (pt0, pt1)[idx % 2]
                            if idx >= 2:
                                t.wait_ge(P.S(EV), base_ev + idx - 1)
                            for s8 in range(8):
                                ins = t.transpose(pt[:, s8 * 128:(s8 + 1) * 128],
                                                  ap(arena, colb + (b * 8 + s8) * 128, 128),
                                                  ap(CSB, IDB, 128))
                            ins.then_inc(P.S(MM), 1)
                            P.cnt[MM] += 1
                    t.wait_ge(P.S(EV), base_ev + ng * 4)
                    for h2 in range(32):
                        t.wait_ge(P.S(W2T), base_w2t + (h2 + 1) * 16)
                        for gi in range(ng):
                            for dt_ in range(2):
                                ins = t.matmul(ps[gi * 2 + dt_][:, :],
                                               ap(A, h2 * (ng * 128) + gi * 128, 128),
                                               ap(Fb, WT + (h2 % 2) * 1024 + dt_ * 512, 512),
                                               start=(h2 == 0), stop=(h2 == 31))
                        ins.then_inc(P.S(MM), 1)
                        P.cnt[MM] += 1
                @blk.vector
                def _(v):
                    for gi, ti in enumerate(grp):
                        for b in range(4):
                            idx = gi * 4 + b
                            pt = (pt0, pt1)[idx % 2]
                            v.wait_ge(P.S(MM), base_mm + idx + 1)
                            dst = bass.AP(A, (b * 8) * (ng * 128) + gi * 128,
                                          [[A.shape[1], 128], [ng * 128, 8], [1, 128]])
                            srcp = bass.AP(pt, 0, [[1024, 128], [128, 8], [1, 128]])
                            ins = v.tensor_copy(dst, srcp)
                            P.inc(ins, EV)
                    for gi, ti in enumerate(grp):
                        v.wait_ge(P.S(MM), base_mm + ng * 4 + 32)
                        yar, ycol = hs_f32(ti)
                        for dt_ in range(2):
                            ins = v.tensor_scalar(ap(yar, ycol + dt_ * 512, 512),
                                                  ps[gi * 2 + dt_][:, :], ap(SM, S_WS + 3, 1),
                                                  1.0, ALU.mult, ALU.mult)
                        v.drain()
                        ins = v.reduce_sum(ap(SM, S_SUM + ti, 1), ap(yar, ycol, 1024), AX.X)
                        v.drain()
                        P.inc(ins, EV)
                @blk.sync
                def _(sy):
                    for h2 in range(32):
                        if h2 >= 2:
                            sy.wait_ge(P.S(MM), base_mm + ng * 4 + h2 - 1)
                        P.inc(sy.dma_start(ap(Fb, WT + (h2 % 2) * 1024, 1024),
                                           bass.AP(w2t_spill, h2 * 128 * 1024, [[1024, 128], [1, 1024]])),
                              W2T, 16)
            for ti in grp:
                yar, ycol = hs_f32(ti)
                with nc.Block() as blk2:
                    base_ac2, base_ev2, base_sy2 = P.val(AC), P.val(EV), P.val(SY)
                    @blk2.sync
                    def _(sy):
                        P.inc(sy.dma_start(ap(Ff32, 1024, 1024),
                                           x_state[ti * 128:(ti + 1) * 128, :]), SY, 16)
                    @blk2.scalar
                    def _(s):
                        ins = s.activation(ap(H, 0, 1024), ap(yar, ycol, 1024), AF.Square,
                                           accum_out=ap(SM, S_SQ + ti, 1))
                        P.inc(ins, AC)
                    @blk2.vector
                    def _(v):
                        v.wait_ge(P.S(AC), base_ac2 + 1)
                        v.tensor_scalar(ap(SM, S_MEAN + ti, 1), ap(SM, S_SUM + ti, 1), 1.0 / D,
                                        0.0, ALU.mult, ALU.add)
                        v.drain()
                        v.tensor_tensor(ap(SM, S_TMP + ti, 1), ap(SM, S_MEAN + ti, 1),
                                        ap(SM, S_MEAN + ti, 1), ALU.mult)
                        v.drain()
                        v.scalar_tensor_tensor(ap(SM, S_VAR + ti, 1), ap(SM, S_SQ + ti, 1), 1.0 / D,
                                               ap(SM, S_TMP + ti, 1), ALU.mult, ALU.subtract)
                        v.drain()
                        v.tensor_scalar(ap(SM, S_VAR + ti, 1), ap(SM, S_VAR + ti, 1), EPS, 0.0,
                                        ALU.add, ALU.add)
                        v.drain()
                        ins = v.reciprocal(ap(SM, S_TMP + ti, 1), ap(SM, S_VAR + ti, 1))
                        P.inc(ins, EV)
                    @blk2.scalar
                    def _(s2):
                        s2.wait_ge(P.S(EV), base_ev2 + 1)
                        ins = s2.activation(ap(SM, S_R2 + ti, 1), ap(SM, S_TMP + ti, 1), AF.Sqrt)
                        P.inc(ins, AC)
                    @blk2.vector
                    def _(v2):
                        v2.wait_ge(P.S(AC), base_ac2 + 2)
                        ins = v2.scalar_tensor_tensor(ap(SM, S_NMR2 + ti, 1), ap(SM, S_MEAN + ti, 1),
                                                      -1.0, ap(SM, S_R2 + ti, 1), ALU.mult, ALU.mult)
                        v2.drain()
                        P.inc(ins, EV)
                    @blk2.scalar
                    def _(s3):
                        s3.wait_ge(P.S(EV), base_ev2 + 2)
                        ins = s3.activation(ap(Ff32, 0, 1024), ap(yar, ycol, 1024), AF.Identity,
                                            bias=ap(SM, S_NMR2 + ti, 1), scale=ap(SM, S_R2 + ti, 1))
                        P.inc(ins, AC)
                    @blk2.vector
                    def _(v3):
                        v3.wait_ge(P.S(AC), base_ac2 + 3)
                        v3.wait_ge(P.S(SY), base_sy2 + 16)
                        ins = v3.tensor_tensor(ap(Ff32, 0, 1024), ap(Ff32, 0, 1024),
                                               ap(Ff32, 1024, 1024), ALU.add)
                        P.inc(ins, EV)
                    @blk2.gpsimd
                    def _(g):
                        g.wait_ge(P.S(EV), P.val(EV))
                        P.inc(g.dma_start(dst_dram[ti * 128:(ti + 1) * 128, :], ap(Ff32, 0, 1024)),
                              ST, 16)
                        g.wait_ge(P.S(ST), P.val(ST))
        if "yC" in tapd:
            emit_tap(tapd["yC"], CF32, 0, 8192)

    emit_ln(x_in)
    if "xn_tm" in tapd:
        emit_tap(tapd["xn_tm"], C, 0, NT * D)
    emit_transpose_tm_fm(C, 0, 1024, A, 0, TOK, NT, NCH)
    if "xn_fm" in tapd:
        emit_tap(tapd["xn_fm"], A, 0, NCH * TOK)
    emit_l0_attn()
    if n_blocks >= 2:
        emit_l0_ffn(out_dst=None)
    if n_blocks >= 3:
        emit_l1_attn(out_dst=None)
    if n_blocks >= 4:
        emit_l1_ffn(out_dst="out")

    if n_blocks <= 3:
        with nc.Block() as blk:
            @blk.gpsimd
            def _(g):
                for ti in range(NT):
                    P.inc(g.dma_start(ap(G, (ti % 4) * 1024, 1024), x_state[ti * 128:(ti + 1) * 128, :]), GQ, 16)
                    g.wait_ge(P.S(GQ), P.val(GQ))
                    P.inc(g.dma_start(out_d[ti * 128:(ti + 1) * 128, :], ap(G, (ti % 4) * 1024, 1024)), ST, 16)
                g.wait_ge(P.S(ST), P.val(ST))

    for gu in P._guards:
        gu.__exit__(None, None, None)
    return nc


def host_consts():
    ident = np.eye(128, dtype=np.float32)
    import ml_dtypes
    identb = ident.astype(ml_dtypes.bfloat16)
    identf = np.eye(2, dtype=np.float32)
    p = np.arange(128)[:, None]
    f = np.arange(512)[None, :]
    bands = []
    for i in range(28):
        d0 = (i - 12) * 128
        bands.append(np.abs(d0 + p - f))
    dm = np.concatenate(bands, axis=1).astype(np.float16)
    return identb, identf, dm


def make_in_maps(inputs, n_cores=8):
    identb, identf, dm = host_consts()
    base = dict(
        w_down=np.ascontiguousarray(inputs["L0_attn_W_down"]),
        tokeys=np.ascontiguousarray(inputs["L0_attn_tokeys"]),
        toquer=np.ascontiguousarray(inputs["L0_attn_toqueries"]),
        tovals=np.ascontiguousarray(inputs["L0_attn_tovalues"]),
        w_uni=np.ascontiguousarray(inputs["L0_attn_W_unify"]),
        b_uni=np.ascontiguousarray(inputs["L0_attn_b_unify"]),
        wg1=np.ascontiguousarray(inputs["L0_ffn_wg1"]),
        bg1=np.ascontiguousarray(inputs["L0_ffn_bg1"]),
        wg2=np.ascontiguousarray(inputs["L0_ffn_wg2"]),
        w_ine=np.ascontiguousarray(inputs["L0_ffn_w_in"]),
        w_oute=np.ascontiguousarray(inputs["L0_ffn_w_out"]),
        wqkv=np.ascontiguousarray(inputs["L1_attn_Wqkv"]),
        bqkv=np.ascontiguousarray(inputs["L1_attn_bqkv"]),
        wout=np.ascontiguousarray(inputs["L1_attn_Wout"]),
        bout=np.ascontiguousarray(inputs["L1_attn_bout"]),
        w1=np.ascontiguousarray(inputs["L1_ffn_w1"]),
        convw=np.ascontiguousarray(inputs["L1_ffn_conv_w"]),
        convb=np.ascontiguousarray(inputs["L1_ffn_conv_b"]),
        w2=np.ascontiguousarray(inputs["L1_ffn_w2"]),
        ident=identb, identf=identf, dmast=dm,
    )
    x = np.asarray(inputs["L0_attn_x"])
    return [dict(base, x=np.ascontiguousarray(x[i])) for i in range(n_cores)]


def kernel(**inputs):
    nc = build(n_blocks=N_BLOCKS, taps=DEBUG_TAPS)
    in_maps = make_in_maps(inputs)
    res = run_bass_kernel_spmd(nc, in_maps, core_ids=list(range(8)))
    out = np.stack([np.asarray(r["out"]) for r in res.results]).astype(np.float32)
    return out
